# version 20
# baseline (speedup 1.0000x reference)
import contextlib

import numpy as np
import ml_dtypes

import concourse.bass as bass
import concourse.mybir as mybir
from concourse.bass_utils import run_bass_kernel_spmd

F32 = mybir.dt.float32
BF16 = mybir.dt.bfloat16
AF = mybir.ActivationFunctionType
ALU = mybir.AluOpType
AX = mybir.AxisListType
BFNP = ml_dtypes.bfloat16

P = 128
DM = 2048
NF = DM // P
H = 16
D = 128
FI = 8192
EPS_LN = 1e-5
EPS_AT = 1e-6

ENGINES = ("sync", "gpsimd", "tensor", "vector", "scalar")



class Emit:

    def __init__(self):
        self.ops = {e: [] for e in ENGINES}
        self.cnt = {}
        self.handles = {}
        self.waited = {e: {} for e in ENGINES}

    def sem(self, name):
        self.cnt[name] = 0

    def val(self, name):
        return self.cnt[name]

    def marks(self):
        return {s: self.cnt[s] for s in ("tt", "vv", "ss")}

    def _mkwait(self, eng, waits):
        real = []
        for s, v in waits:
            if s is None or v is None or v <= 0:
                continue
            if self.waited[eng].get(s, -1) >= v:
                continue
            self.waited[eng][s] = v
            real.append((s, v))
        return real

    def op(self, eng, build, sem=None, inc=1, waits=()):
        real = self._mkwait(eng, waits)
        v = None
        if sem is not None:
            self.cnt[sem] += inc
            v = self.cnt[sem]
        hs = self.handles

        def f(e, build=build, real=real, sem=sem, inc=inc):
            for s, vv in real:
                e.wait_ge(hs[s], vv)
            ins = build(e)
            if sem is not None:
                ins.then_inc(hs[sem], inc)

        self.ops[eng].append(f)
        return v

    def wait(self, eng, waits):
        real = self._mkwait(eng, waits)
        hs = self.handles
        if not real:
            return

        def f(e, real=real):
            for s, vv in real:
                e.wait_ge(hs[s], vv)

        self.ops[eng].append(f)


def build_program(R=1024, dumps=(), phases_upto="P5"):
    NT = R // P
    NCH = R // P
    TB = 512 if R >= 512 else R
    NTB = max(R // TB, 1)

    nc = bass.Bass("TRN2", target_bir_lowering=False, debug=False,
                   detect_race_conditions=False)

    x_h = nc.dram_tensor("x", [R, DM], F32, kind="ExternalInput").ap()
    flag_h = nc.dram_tensor("flag", [P, 2], F32, kind="ExternalInput").ap()
    wq_h = nc.dram_tensor("wq", [DM, DM], F32, kind="ExternalInput").ap()
    wk_h = nc.dram_tensor("wk", [DM, DM], F32, kind="ExternalInput").ap()
    wv_h = nc.dram_tensor("wv", [DM, DM], F32, kind="ExternalInput").ap()
    wo_h = nc.dram_tensor("wo", [DM, DM], F32, kind="ExternalInput").ap()
    bq_h = nc.dram_tensor("bq", [1, DM], F32, kind="ExternalInput").ap()
    bk_h = nc.dram_tensor("bk", [1, DM], F32, kind="ExternalInput").ap()
    bv_h = nc.dram_tensor("bv", [1, DM], F32, kind="ExternalInput").ap()
    bo_h = nc.dram_tensor("bo", [1, DM], F32, kind="ExternalInput").ap()
    wg_h = nc.dram_tensor("wg", [DM, FI], F32, kind="ExternalInput").ap()
    wu_h = nc.dram_tensor("wu", [DM, FI], F32, kind="ExternalInput").ap()
    wd_h = nc.dram_tensor("wd", [FI, DM], F32, kind="ExternalInput").ap()
    out_h = nc.dram_tensor("out", [R, DM], F32, kind="ExternalOutput").ap()
    cc_in_h = nc.dram_tensor("cc_in", [P, H * (D + 1)], BF16).ap()
    cc_out_h = nc.dram_tensor("cc_out", [P, H * (D + 1)], BF16).ap()

    x_r = x_h.rearrange("(t p) f -> p t f", p=P)
    out_r = out_h.rearrange("(t p) f -> p t f", p=P)
    wq_r = wq_h.rearrange("(k p) n -> p k n", p=P)
    wk_r = wk_h.rearrange("(k p) n -> p k n", p=P)
    wv_r = wv_h.rearrange("(k p) n -> p k n", p=P)
    wo_r = wo_h.rearrange("(k p) n -> p k n", p=P)
    wg_r = wg_h.rearrange("(k p) n -> p k n", p=P)
    wu_r = wu_h.rearrange("(k p) n -> p k n", p=P)
    wd_r = wd_h.rearrange("(s p) n -> p s n", p=P)

    ident_c = nc.inline_tensor(np.eye(P, dtype=BFNP), name="identc").ap()
    identf_c = nc.inline_tensor(np.eye(16, dtype=np.float32), name="identfc").ap()
    mask_c = nc.inline_tensor(np.triu(np.ones((P, P), dtype=np.float32)),
                              name="maskc").ap()

    A = {}

    BASE0 = (nc.sbuf_base + 31) & ~31
    assert BASE0 + 211712 <= nc.sbuf_top, (BASE0, nc.sbuf_top)

    def sb(name, off, shape, dtype):
        A[name] = nc.alloc_sbuf_tensor_at(name, shape, dtype, offset=BASE0 + off).ap()

    sb("ident", 0, [P, P], BF16)
    sb("mask", 256, [P, P], F32)
    sb("bqt", 768, [P, NF], F32)
    sb("bkt", 832, [P, NF], F32)
    sb("bvt", 896, [P, NF], F32)
    sb("flag", 960, [P, 2], F32)
    sb("ones_row", 992, [1, 512], F32)
    sb("b_row", 3040, [1, DM], F32)
    sb("rep", 11232, [P, 512], F32)
    sb("b16", 11232, [16, P], F32)
    sb("e_buf", 13280, [P, 2, 512], F32)
    S_OFF = 17376
    sb("S", S_OFF, [P, H, D + 1], F32)
    sb("sqbuf", S_OFF, [P, 512], F32)
    sb("geb", S_OFF + 2048, [P, 512], F32)
    sb("hsp_bf", S_OFF + 4096, [P, 2, 512], BF16)
    sb("S_bf", 25632, [P, H, D + 1], BF16)
    sb("kf_tok", 29760, [P, 4, P], BF16)
    sb("identf", 30784, [16, 16], F32)
    sb("sum_t", 31360, [P, 8], F32)
    sb("sumsq_t", 31392, [P, 8], F32)
    sb("m_t", 31424, [P, 8], F32)
    sb("v_t", 31456, [P, 8], F32)
    sb("rstd_t", 31488, [P, 8], F32)
    sb("z_sb", 31552, [P, NCH * H], F32)
    sb("wsc", 32064, [1, 64], F32)
    R1 = 32768
    sb("x_stage", R1, [P, 2, DM], F32)
    sb("xn_stage", R1 + 16384, [P, DM], BF16)
    sb("trash", R1 + 20480, [P, DM], BF16)
    sb("wtile", R1 + 16384, [P, 4, NF, P], BF16)
    sb("wvbuf", R1, [P, 2, NF, 512], BF16)
    sb("numer", R1, [P, NCH, H, P], BF16)
    sb("wubuf", R1, [P, 2, NF, 512], BF16)
    sb("wdbuf", R1, [P, 2, 16, 512], BF16)
    R2 = R1 + 32768
    sb("xnT", R2, [P, NF, R], BF16)
    R3 = R2 + 32768
    sb("qfT", R3, [P, NF, R], BF16)
    sb("xfullA", R3, [P, 4, DM], F32)
    sb("wgbuf", R3, [P, 2, NF, 512], BF16)
    sb("wdbuf2", R3, [P, 2, 16, 512], BF16)
    R4 = R3 + 32768
    sb("kfT", R4, [P, NF, R], BF16)
    sb("xfullB", R4, [P, 4, DM], F32)
    sb("outnT", R4, [P, NF, R], BF16)
    sb("hspT_b", R4, [P, 16, R], BF16)
    R5 = R4 + 32768
    sb("vaug", R5, [P, NCH, H, D + 1], BF16)
    sb("wobuf", R5, [P, 2, NF, 512], BF16)
    sb("hspT_a", R5, [P, 16, R], BF16)
    EX = R5 + NCH * H * (D + 1) * 2
    sb("dstage", EX, [P, 2, 512], F32)
    sb("obuf", EX + 4096, [P, 2, 512], F32)
    sb("m1", EX + 8192, [P, P], F32)
    sb("n1", EX + 8704, [P, P], F32)
    sb("at_m", EX + 8192, [P, 4, P], BF16)
    sb("m2", EX + 9216, [P, P], F32)
    sb("n2", EX + 9728, [P, P], F32)
    sb("tq", EX + 10240, [P, P], F32)
    sb("mx_buf", EX + 10752, [P, 2, 512], F32)

    pb = [nc.alloc_psum_tensor(f"pb{i}", [P, 512], F32) for i in range(8)]
    pbF = [t.ap() for t in pb]
    pbB = [t.ap().bitcast(BF16) for t in pb]

    ap = A
    E = Emit()
    for s in ("tt", "vv", "ss", "xc", "xb", "xs0", "xs1", "xs2",
              "sb2", "sb3", "xe0", "xe1",
              "xo0", "xo1", "wt0", "wt1", "wt2", "wt3",
              "wv0", "wv1", "wv2", "wv3",
              "wo0", "wo1", "wo2", "wo3",
              "wg0", "wg1", "wu0", "wu1", "wd0", "wd1", "wd2", "wd3",
              "sb0", "sb1", "sd",
              "cci", "ccs", "cco"):
        E.sem(s)

    bank_waits = [[] for _ in range(8)]

    def t_first(bank, build, waits=(), sem=None):
        w = list(waits) + bank_waits[bank]
        bank_waits[bank] = []
        return E.op("tensor", build, sem=sem, waits=w)

    def t_more(build, waits=(), sem=None):
        return E.op("tensor", build, sem=sem, waits=waits)

    def consume(bank, sem, v):
        bank_waits[bank].append((sem, v))

    dump_outs = {}

    def dump(name, src_ap, marks):
        if name not in dumps:
            return
        dh = nc.dram_tensor("dump_" + name, list(src_ap.shape),
                            src_ap.dtype, kind="ExternalOutput").ap()
        dump_outs[name] = dh
        E.op("sync", lambda e: e.dma_start(out=dh, in_=src_ap),
             sem="sd", inc=16,
             waits=[(k, v) for k, v in marks.items() if k in ("tt", "vv", "ss")])

    E.op("sync", lambda e: e.dma_start(out=ap["ident"], in_=ident_c), sem="xc", inc=16)
    E.op("sync", lambda e: e.dma_start(out=ap["mask"], in_=mask_c), sem="xc", inc=16)
    E.op("sync", lambda e: e.dma_start(out=ap["flag"], in_=flag_h), sem="xc", inc=16)
    E.op("sync", lambda e: e.dma_start(out=ap["identf"], in_=identf_c), sem="xc", inc=16)
    consts_xl = E.val("xc")
    E.op("vector", lambda e: e.memset(ap["ones_row"], 1.0), sem="vv")
    E.op("vector", lambda e: e.memset(ap["S"][:, :, :], 0.0), sem="vv")
    base_vv = E.val("vv")

    def build_bias_tile(bh, dst):
        lv = E.op("sync", lambda e, bh=bh: e.dma_start(
            out=ap["b16"], in_=bh.rearrange("o (h c) -> (o h) c", c=P)),
            sem="xb", inc=16, waits=[("tt", E.val("tt"))])
        tv = t_first(0, lambda e: e.transpose(
            pbF[0][:, 0:H], ap["b16"], ap["identf"]),
            waits=[("xb", lv), ("xc", consts_xl)], sem="tt")
        vv = E.op("vector", lambda e, dst=dst: e.tensor_copy(ap[dst], pbF[0][:, 0:H]),
                  sem="vv", waits=[("tt", tv)])
        consume(0, "vv", vv)

    build_bias_tile(bq_h, "bqt")
    build_bias_tile(bk_h, "bkt")
    build_bias_tile(bv_h, "bvt")

    brow = {"xl": None}

    def load_brow(bh, waits=None):
        if waits is None:
            waits = [("tt", E.val("tt"))]
        brow["xl"] = E.op("sync", lambda e, bh=bh: e.dma_start(out=ap["b_row"], in_=bh),
                          sem="xb", inc=16, waits=waits)

    load_brow(bv_h)

    def build_rep(blk):
        tv = t_first(1, lambda e, blk=blk: e.matmul(
            pbF[1][:, 0:512], ap["ones_row"][0:1, 0:P],
            ap["b_row"][0:1, blk * 512:(blk + 1) * 512], start=True, stop=True),
            waits=[("xb", brow["xl"]), ("vv", E.val("vv"))], sem="tt")
        vv = E.op("vector", lambda e: e.tensor_copy(ap["rep"], pbF[1][:, 0:512]),
                  sem="vv", waits=[("tt", tv)])
        consume(1, "vv", vv)
        return vv

    def warm(func, tgt):
        E.op("scalar", lambda e, func=func: e.preload_act_table(func))
        E.op("scalar", lambda e, func=func, tgt=tgt: e.activation(
            tgt, ap["ones_row"][0:1, 0:64], func), sem="ss",
            waits=[("vv", base_vv)])

    ln_state = {
        "xfree": [(None, 0), (None, 0)],
        "tile_vv": [],
    }

    def ln_phase(xv):
        warm(AF.Square, ap["wsc"])
        last_ss = 0
        last_vv = 0
        xs_val = []
        qmap = [("sync", "xs0"), ("scalar", "xs1")]
        for t in range(NT):
            st = ap["xfullA"][:, t, :] if t < 4 else ap["xfullB"][:, t - 4, :]
            qe, qs = qmap[t % 2]
            xl = E.op(qe, lambda e, st=st, t=t: e.dma_start(
                out=st, in_=xv[:, t, :]), sem=qs, inc=16)
            xs_val.append((qs, xl))
        for t in range(NT):
            st = ap["xfullA"][:, t, :] if t < 4 else ap["xfullB"][:, t - 4, :]
            last_ss = E.op("scalar", lambda e, st=st, t=t: e.activation(
                ap["trash"], st, AF.Square, accum_out=ap["sumsq_t"][:, t:t + 1]),
                sem="ss", waits=[xs_val[t]])
            last_vv = E.op("vector", lambda e, st=st, t=t: e.tensor_reduce(
                ap["sum_t"][:, t:t + 1], st, AX.X, ALU.add),
                sem="vv", waits=[xs_val[t]])
        v1 = E.op("vector", lambda e: e.tensor_scalar(
            ap["m_t"][:, 0:NT], ap["sum_t"][:, 0:NT], 1.0 / DM, None, ALU.mult),
            sem="vv", waits=[("vv", last_vv), ("ss", last_ss)])
        v2 = E.op("vector", lambda e: e.tensor_scalar(
            ap["v_t"][:, 0:NT], ap["sumsq_t"][:, 0:NT], 1.0 / DM, None, ALU.mult),
            sem="vv", waits=[("vv", v1)])
        v3 = E.op("vector", lambda e: e.tensor_tensor(
            ap["rstd_t"][:, 0:NT], ap["m_t"][:, 0:NT], ap["m_t"][:, 0:NT],
            ALU.mult), sem="vv", waits=[("vv", v2)])
        v4 = E.op("vector", lambda e: e.tensor_tensor(
            ap["v_t"][:, 0:NT], ap["v_t"][:, 0:NT], ap["rstd_t"][:, 0:NT],
            ALU.subtract), sem="vv", waits=[("vv", v3)])
        v5 = E.op("vector", lambda e: e.tensor_scalar(
            ap["v_t"][:, 0:NT], ap["v_t"][:, 0:NT], EPS_LN, None, ALU.add),
            sem="vv", waits=[("vv", v4)])
        warm(AF.Sqrt, ap["wsc"])
        sq = E.op("scalar", lambda e: e.activation(
            ap["rstd_t"][:, 0:NT], ap["v_t"][:, 0:NT], AF.Sqrt),
            sem="ss", waits=[("vv", v5)])
        rv = E.op("vector", lambda e: e.reciprocal(
            ap["rstd_t"][:, 0:NT], ap["rstd_t"][:, 0:NT]),
            sem="vv", waits=[("ss", sq)])
        tile_tt = []
        for t in range(NT):
            st = ap["xfullA"][:, t, :] if t < 4 else ap["xfullB"][:, t - 4, :]
            xnname = "xn_stage" if t % 2 == 0 else "trash"
            xw = [("vv", rv)] + ([tile_tt[t - 2]] if t >= 2 else [])
            xnv = E.op("vector", lambda e, st=st, t=t, xnname=xnname: e.tensor_scalar(
                ap[xnname], st, ap["m_t"][:, t:t + 1],
                ap["rstd_t"][:, t:t + 1], ALU.subtract, ALU.mult),
                sem="vv", waits=xw)
            for half in range(2):
                bank = 6 + half
                for j in range(8):
                    k = half * 8 + j
                    bld = (lambda e, k=k, j=j, bank=bank, xnname=xnname: e.transpose(
                        pbB[bank][:, j * P:(j + 1) * P],
                        ap[xnname][:, k * P:(k + 1) * P], ap["ident"]))
                    if j == 0:
                        tv = t_first(bank, bld, waits=[("vv", xnv), ("xc", consts_xl)])
                    else:
                        tv = t_more(bld, sem="tt" if j == 7 else None)
                dst = ap["xnT"][:, half * 8:(half + 1) * 8, t * P:(t + 1) * P]
                src = pbB[bank].rearrange("p (j c) -> p j c", c=P)[:, 0:8, :]
                sv = E.op("scalar", lambda e, dst=dst, src=src: e.copy(dst, src),
                          sem="ss", waits=[("tt", tv)])
                consume(bank, "ss", sv)
            tile_tt.append(("tt", tv))
            ln_state["tile_vv"].append(("ss", sv))

    wtile_free = [(None, 0)] * 4
    wtile_order = [2, 3, 0, 1]
    ebuf_free = [(None, 0), (None, 0)]

    def proj_feat(w_r_view, bias, dst, xn_marks):
        evac = {}
        tiles = ln_state["tile_vv"]
        for m in range(NF):
            sl = wtile_order[m % 4]
            w = [wtile_free[sl]] if wtile_free[sl][0] else []
            if sl < 2 and wtile_free[sl][0] is None:
                w = [tiles[-1]]
            lva = E.op("gpsimd", lambda e, m=m, sl=sl: e.dma_start(
                out=ap["wtile"][:, sl, :, :],
                in_=w_r_view[:, :, m * P:(m + 1) * P]),
                sem=f"wt{sl}", inc=16, waits=w)
            for tb in range(NTB):
                u = m * NTB + tb
                bank = u % 2
                w2 = [(f"wt{sl}", lva), tiles[min(tb * 4 + 3, len(tiles) - 1)]]
                for k in range(NF):
                    bld = (lambda e, k=k, sl=sl, tb=tb, bank=bank: e.matmul(
                        pbF[bank][:, 0:TB], ap["wtile"][:, sl, k, :],
                        ap["xnT"][:, k, tb * TB:(tb + 1) * TB],
                        start=(k == 0), stop=(k == NF - 1)))
                    if k == 0:
                        tv = t_first(bank, bld, waits=w2)
                    else:
                        tv = t_more(bld, sem="tt" if k == NF - 1 else None)
                esl = u % 2
                dslice = ap[dst][:, m, tb * TB:(tb + 1) * TB]
                mw = [("ss", evac[u - 2][0][1])] if (u - 2) in evac else []
                v1 = E.op("vector", lambda e, bank=bank, esl=esl, m=m: e.tensor_scalar(
                    ap["mx_buf"][:, esl, 0:TB], pbF[bank][:, 0:TB],
                    ap[bias][:, m:m + 1], None, ALU.add),
                    sem="vv", waits=[("tt", tv)] + mw)
                ew = [ebuf_free[esl]] if ebuf_free[esl][0] else []
                s1 = E.op("scalar", lambda e, esl=esl: e.activation(
                    ap["e_buf"][:, esl, 0:TB], ap["mx_buf"][:, esl, 0:TB], AF.Exp),
                    sem="ss", waits=[("vv", v1)] + ew)
                v2a = E.op("vector", lambda e, esl=esl: e.tensor_scalar(
                    ap["mx_buf"][:, esl, 0:TB], ap["mx_buf"][:, esl, 0:TB],
                    0.0, None, ALU.max), waits=[("ss", s1)])
                v2 = E.op("vector", lambda e, esl=esl, dslice=dslice:
                          e.scalar_tensor_tensor(
                              dslice, ap["e_buf"][:, esl, 0:TB], 1.0,
                              ap["mx_buf"][:, esl, 0:TB], ALU.min, ALU.add),
                          sem="vv")
                evac[u] = [("ss", s1), ("vv", v2)]
                ebuf_free[esl] = ("vv", v2)
                consume(bank, "vv", v1)
            wtile_free[sl] = ("tt", E.val("tt"))
        return E.marks()

    wv_free = [(None, 0), (None, 0)]

    def v_pass(xn_marks, vaug_guard, r1_guards):
        ones_w = ([vaug_guard] if vaug_guard[0] else [])
        ov = E.op("vector", lambda e: e.memset(ap["vaug"][:, :, :, D:D + 1], 1.0),
                  sem="vv", waits=ones_w)
        tiles = ln_state["tile_vv"]
        for blk in range(4):
            sl = blk % 2
            w = ([wv_free[sl]] if wv_free[sl][0] else []) + list(r1_guards[sl])
            lva = E.op("gpsimd", lambda e, blk=blk, sl=sl: e.dma_start(
                out=ap["wvbuf"][:, sl, :, :],
                in_=wv_r[:, :, blk * 512:(blk + 1) * 512]),
                sem=f"wv{sl}", inc=16, waits=w)
            rep_vv = build_rep(blk)
            for ct in range(NT):
                u = blk * NT + ct
                bank = u % 2
                w2 = [(f"wv{sl}", lva), tiles[min(ct, len(tiles) - 1)]]
                for k in range(NF):
                    bld = (lambda e, k=k, sl=sl, ct=ct, bank=bank: e.matmul(
                        pbF[bank][:, 0:512], ap["xnT"][:, k, ct * P:(ct + 1) * P],
                        ap["wvbuf"][:, sl, k, :],
                        start=(k == 0), stop=(k == NF - 1)))
                    if k == 0:
                        tv = t_first(bank, bld, waits=w2)
                    else:
                        tv = t_more(bld, sem="tt" if k == NF - 1 else None)
                last_v = None
                for j in range(4):
                    h = 4 * blk + j
                    last_v = E.op("vector", lambda e, bank=bank, j=j, ct=ct, h=h:
                                  e.tensor_tensor(
                                      ap["vaug"][:, ct, h, 0:D],
                                      pbF[bank][:, j * P:(j + 1) * P],
                                      ap["rep"][:, j * P:(j + 1) * P], ALU.add),
                                  sem="vv",
                                  waits=[("tt", tv), ("vv", rep_vv), ("vv", ov)])
                consume(bank, "vv", last_v)
            wv_free[sl] = ("tt", E.val("tt"))
        return E.marks()

    ln_phase(x_r)
    m_xn = E.marks()
    dump("xnT", ap["xnT"], m_xn)
    if phases_upto == "Pa":
        finalize(nc, E)
        return nc, dump_outs

    warm(AF.Exp, ap["wsc"])
    m_q = proj_feat(wq_r, "bqt", "qfT", m_xn)
    dump("qfT", ap["qfT"], m_q)
    m_k = proj_feat(wk_r, "bkt", "kfT", m_xn)
    dump("kfT", ap["kfT"], m_k)
    m_v = v_pass(m_k, (None, 0),
                 ([("vv", m_xn["vv"])],
                  [("tt", m_k["tt"]), ("ss", m_k["ss"]), ("vv", m_k["vv"])]))
    pb_marks = E.marks()
    dump("vaug", ap["vaug"], pb_marks)

    if phases_upto == "Pb":
        finalize(nc, E)
        return nc, dump_outs

    E.wait("vector", [("tt", pb_marks["tt"]), ("ss", pb_marks["ss"])])
    sbf_val = {}
    pc_state = {}
    kft_free = [(None, 0)] * 4
    atm_free = [(None, 0)] * 4
    state_q = []

    def emit_state(item):
        j, n, h, s4, p2, vc, v2 = item
        if n > 0:
            t_first(p2, lambda e, n=n, h=h, p2=p2: e.matmul(
                pbF[p2][:, 0:D + 1], ap["qfT"][:, h, n * P:(n + 1) * P],
                ap["S_bf"][:, h, :], start=True, stop=False),
                waits=[sbf_val[h]])
            tv4 = t_more(lambda e, n=n, h=h, s4=s4, p2=p2: e.matmul(
                pbF[p2][:, 0:D + 1], ap["at_m"][:, s4, :],
                ap["vaug"][:, n, h, :], start=False, stop=True),
                waits=[("vv", v2)], sem="tt")
        else:
            tv4 = t_first(p2, lambda e, n=n, h=h, s4=s4, p2=p2: e.matmul(
                pbF[p2][:, 0:D + 1], ap["at_m"][:, s4, :],
                ap["vaug"][:, n, h, :], start=True, stop=True),
                waits=[("vv", v2)], sem="tt")
        sbank = 2 if p2 == 0 else 7
        tv5 = t_first(sbank, lambda e, s4=s4, n=n, h=h, sbank=sbank: e.matmul(
            pbF[sbank][:, 0:D + 1], ap["kf_tok"][:, s4, :],
            ap["vaug"][:, n, h, :], start=True, stop=True),
            waits=[("ss", vc)], sem="tt")
        v_num = E.op("vector", lambda e, p2=p2, n=n, h=h: e.tensor_copy(
            ap["numer"][:, n, h, :], pbF[p2][:, 0:D]),
            sem="vv", waits=[("tt", tv4)])
        s_zev = E.op("scalar", lambda e, p2=p2, i=n * H + h: e.copy(
            ap["z_sb"][:, i:i + 1], pbF[p2][:, D:D + 1]),
            sem="ss", waits=[("tt", tv4), ("vv", v_num)])
        consume(p2, "vv", v_num)
        consume(p2, "ss", s_zev)
        atm_free[s4] = ("tt", tv4)
        va = E.op("vector", lambda e, h=h, sbank=sbank: e.tensor_tensor(
            ap["S"][:, h, :], ap["S"][:, h, :], pbF[sbank][:, 0:D + 1], ALU.add),
            sem="vv", waits=[("tt", tv5)])
        consume(sbank, "vv", va)
        kft_free[s4] = ("tt", tv5)
        if n < NCH - 1:
            vb = E.op("vector", lambda e, h=h: e.tensor_copy(
                ap["S_bf"][:, h, :], ap["S"][:, h, :]), sem="vv",
                waits=[("vv", va)])
            sbf_val[h] = ("vv", vb)
        if n == 3 and h == H - 1:
            pc_state["vaug03_tt"] = E.val("tt")

    for n in range(NCH):
        for h in range(H):
            i = n * H + h
            s4 = i % 4
            p2 = i % 2
            if len(state_q) >= 4:
                emit_state(state_q.pop(0))
            tb_bank = 5 + p2
            tv1 = t_first(tb_bank, lambda e, n=n, h=h, tb_bank=tb_bank: e.transpose(
                pbB[tb_bank][:, 0:P], ap["kfT"][:, h, n * P:(n + 1) * P],
                ap["ident"]), waits=[("vv", pb_marks["vv"])], sem="tt")
            kw = [kft_free[s4]] if kft_free[s4][0] else []
            vc = E.op("scalar", lambda e, s4=s4, tb_bank=tb_bank: e.copy(
                ap["kf_tok"][:, s4, :], pbB[tb_bank][:, 0:P]),
                sem="ss", waits=[("tt", tv1)] + kw)
            consume(tb_bank, "ss", vc)
            ab = 3 + p2
            tv2 = t_first(ab, lambda e, n=n, h=h, ab=ab: e.matmul(
                pbF[ab][:, 0:P], ap["kfT"][:, h, n * P:(n + 1) * P],
                ap["qfT"][:, h, n * P:(n + 1) * P], start=True, stop=True), sem="tt")
            aw = [atm_free[s4]] if atm_free[s4][0] else []
            v2 = E.op("vector", lambda e, s4=s4, ab=ab: e.tensor_tensor(
                ap["at_m"][:, s4, :], pbF[ab][:, 0:P], ap["mask"], ALU.mult),
                sem="vv", waits=[("tt", tv2)] + aw)
            consume(ab, "vv", v2)
            state_q.append((i, n, h, s4, p2, vc, v2))
    while state_q:
        emit_state(state_q.pop(0))
    pc_marks = E.marks()
    dump("numer", ap["numer"], pc_marks)
    dump("Sloc", ap["S"], pc_marks)

    if phases_upto == "Pc":
        finalize(nc, E)
        return nc, dump_outs

    wo_lv = {}
    wo_lv[0] = E.op("gpsimd", lambda e: e.dma_start(
        out=ap["wobuf"][:, 0, :, :], in_=wo_r[:, :, 0:512]),
        sem="wo0", inc=16, waits=[("tt", pc_state["vaug03_tt"])])
    S_flat = ap["S"].rearrange("p h d -> p (h d)")
    Sbf_flat = ap["S_bf"].rearrange("p h d -> p (h d)")
    vfl = E.op("vector", lambda e: e.tensor_scalar(
        Sbf_flat, S_flat, ap["flag"][:, 1:2], None, ALU.mult),
        sem="vv", waits=[("xc", consts_xl), ("vv", pc_marks["vv"])])
    cl = E.op("sync", lambda e: e.dma_start(out=cc_in_h, in_=Sbf_flat),
              sem="cci", inc=16, waits=[("vv", vfl)])
    E.op("gpsimd", lambda e: e.collective_compute(
        "AllReduce", ALU.add,
        replica_groups=[[0, 1], [2, 3], [4, 5], [6, 7]],
        ins=[cc_in_h.opt()], outs=[cc_out_h.opt()]),
        sem="ccs", inc=1, waits=[("cci", cl)])
    co = E.op("sync", lambda e: e.dma_start(out=Sbf_flat, in_=cc_out_h),
              sem="cco", inc=16, waits=[("ccs", 1)])
    wo_lv[1] = E.op("gpsimd", lambda e: e.dma_start(
        out=ap["wobuf"][:, 1, :, :], in_=wo_r[:, :, 512:1024]),
        sem="wo1", inc=16, waits=[("tt", pc_marks["tt"])])
    sbf_pre = E.op("vector", lambda e: e.tensor_scalar(
        Sbf_flat, Sbf_flat, ap["flag"][:, 0:1], None, ALU.mult),
        sem="vv", waits=[("cco", co)])
    dump("Spre", ap["S_bf"], E.marks())

    load_brow(bo_h, waits=[("tt", pb_marks["tt"])])

    def build_rep_pf(blk, dst, extra):
        tv = t_first(5, lambda e, blk=blk: e.matmul(
            pbF[5][:, 0:512], ap["ones_row"][0:1, 0:P],
            ap["b_row"][0:1, blk * 512:(blk + 1) * 512], start=True, stop=True),
            waits=[("xb", brow["xl"])], sem="tt")
        vv = E.op("vector", lambda e, dst=dst: e.tensor_copy(ap[dst], pbF[5][:, 0:512]),
                  sem="vv", waits=[("tt", tv)] + extra)
        consume(5, "vv", vv)
        return vv

    wo_free = [(None, 0), (None, 0)]
    xslot = [ap["e_buf"][:, 0, :], ap["e_buf"][:, 1, :],
             ap["obuf"][:, 0, :], ap["obuf"][:, 1, :]]
    xsem = ["xe0", "xe1", "xo0", "xo1"]
    xst_free = [[ebuf_free[0]] if ebuf_free[0][0] else [],
                [ebuf_free[1]] if ebuf_free[1][0] else [],
                [], []]
    pf_banks = [3, 4]
    vxb_last = {}
    dst_free = [(None, 0), (None, 0), (None, 0), (None, 0)]
    dslot = [ap["dstage"][:, 0, :], ap["dstage"][:, 1, :],
             ap["mx_buf"][:, 0, :], ap["mx_buf"][:, 1, :]]
    dsem = ["sb0", "sb1", "sb2", "sb3"]
    tile_store = {}
    rep_vv_fb = {}
    pf_lv = {}
    outn_done = {}

    wo23_lv = {}

    def pf_prolog(fb):
        sl = fb % 2
        pf_lv[fb] = wo_lv[fb] if fb in wo_lv else wo23_lv[fb]
        repdst = ("rep", "sqbuf")[fb % 2]
        if fb == 0:
            rg = [("vv", pb_marks["vv"])]
        elif fb == 1:
            rg = [("vv", vfl)]
        else:
            rg = [("vv", vxb_last[fb - 2])]
        rep_vv_fb[fb] = (repdst, build_rep_pf(fb, repdst, rg))

    def pf_tile(fb, ct):
        sl = fb % 2
        lva = pf_lv[fb]
        wname = "wobuf" if fb < 2 else "wgbuf"
        wsem = f"wo{sl}" if fb < 2 else f"wo{fb}"
        repdst, rep_vv = rep_vv_fb[fb]
        u = fb * NT + ct
        xsl = u % 4
        xw = xst_free[xsl]
        xl = E.op("sync", lambda e, ct=ct, fb=fb, xsl=xsl: e.dma_start(
            out=xslot[xsl], in_=x_r[:, ct, fb * 512:(fb + 1) * 512]),
            sem=xsem[xsl], inc=16, waits=xw)
        vxb = E.op("vector", lambda e, xsl=xsl, repdst=repdst: e.tensor_tensor(
            xslot[xsl], xslot[xsl], ap[repdst], ALU.add),
            sem="vv", waits=[(xsem[xsl], xl), ("vv", rep_vv)])
        vxb_last[fb] = vxb
        bank = pf_banks[u % 2]
        w2 = [(wsem, lva), outn_done[ct]]
        for k in range(NF):
            bld = (lambda e, k=k, sl=sl, ct=ct, bank=bank, wname=wname: e.matmul(
                pbF[bank][:, 0:512], ap["outnT"][:, k, ct * P:(ct + 1) * P],
                ap[wname][:, sl, k, :], start=(k == 0), stop=(k == NF - 1)))
            if k == 0:
                tv = t_first(bank, bld, waits=w2)
            else:
                tv = t_more(bld, sem="tt" if k == NF - 1 else None)
        dsl = u % 2 if fb == 0 else u % 4
        if dsl >= 2 and dst_free[dsl][0] is None:
            dw = [mx_free[dsl - 2]]
        else:
            dw = [dst_free[dsl]] if dst_free[dsl][0] else []
        dsv = dslot[dsl]
        vst = E.op("vector", lambda e, bank=bank, dsv=dsv, xsl=xsl: e.tensor_tensor(
            dsv, pbF[bank][:, 0:512], xslot[xsl], ALU.add),
            sem="vv", waits=[("tt", tv), ("vv", vxb)] + dw)
        consume(bank, "vv", vst)
        xst_free[xsl] = [("vv", vst)]
        sv = E.op("sync" if fb == 0 else "scalar",
                  lambda e, ct=ct, fb=fb, dsv=dsv: e.dma_start(
                      out=out_r[:, ct, fb * 512:(fb + 1) * 512], in_=dsv),
                  sem=dsem[dsl], inc=16, waits=[("vv", vst)])
        dst_free[dsl] = (dsem[dsl], sv)
        tile_store[(ct, fb)] = (dsem[dsl], sv)

    pf_prolog(0)
    pf_prolog(1)

    E.wait("scalar", [("tt", pc_marks["tt"])])
    E.wait("vector", [("tt", pc_marks["tt"]), ("ss", pc_marks["ss"])])
    zb = 2
    tvz = None
    for n in range(NCH):
        for h in range(H):
            i = n * H + h
            bld = (lambda e, n=n, h=h, i=i: e.matmul(
                pbF[zb][:, i:i + 1], ap["qfT"][:, h, n * P:(n + 1) * P],
                ap["S_bf"][:, h, D:D + 1], start=True, stop=True))
            if i == 0:
                tvz = t_first(zb, bld, waits=[("vv", sbf_pre)])
            else:
                tvz = t_more(bld, sem="tt" if i == NCH * H - 1 else None)
    v_zf = E.op("vector", lambda e: e.scalar_tensor_tensor(
        ap["z_sb"][:, 0:NCH * H], pbF[zb][:, 0:NCH * H], EPS_AT,
        ap["z_sb"][:, 0:NCH * H], ALU.add, ALU.add),
        sem="vv", waits=[("tt", tvz)])
    consume(zb, "vv", v_zf)
    v_rzf = E.op("vector", lambda e: e.reciprocal(
        ap["n1"][:, 0:NCH * H], ap["z_sb"][:, 0:NCH * H]),
        sem="vv", waits=[("vv", v_zf)])
    mx_free = [(None, 0), (None, 0)]
    hspb_free = [(None, 0), (None, 0)]
    mx4 = ap["mx_buf"].rearrange("p s (j c) -> p s j c", c=P)
    pe_pend = []

    def emit_pe_transpose(item):
        n, j4, sl, sc_marks = item
        tvt = None
        for jj in range(4):
            bld = (lambda e, jj=jj, sl=sl: e.transpose(
                pbB[6 + sl][:, jj * P:(jj + 1) * P],
                ap["hsp_bf"][:, sl, jj * P:(jj + 1) * P], ap["ident"]))
            if jj == 0:
                tvt = t_first(6 + sl, bld, waits=sc_marks)
            else:
                tvt = t_more(bld, sem="tt" if jj == 3 else None)
        hspb_free[sl] = ("tt", tvt)
        dsts = ap["outnT"][:, 4 * j4:4 * j4 + 4, n * P:(n + 1) * P]
        srcs = pbB[6 + sl].rearrange("p (j c) -> p j c", c=P)[:, 0:4, :]
        s_ot = E.op("scalar", lambda e, dsts=dsts, srcs=srcs: e.copy(
            dsts, srcs), sem="ss", waits=[("tt", tvt)])
        consume(6 + sl, "ss", s_ot)
        if j4 == 3:
            outn_done[n] = ("ss", s_ot)

    for n in range(NCH):
        for j4 in range(4):
            g = n * 4 + j4
            sl = g % 2
            tva = None
            for jj in range(4):
                h = j4 * 4 + jj
                bld = (lambda e, n=n, h=h, jj=jj, sl=sl: e.matmul(
                    pbF[sl][:, jj * P:(jj + 1) * P],
                    ap["qfT"][:, h, n * P:(n + 1) * P],
                    ap["S_bf"][:, h, 0:D], start=True, stop=True))
                if jj == 0:
                    tva = t_first(sl, bld, waits=[("vv", sbf_pre)])
                else:
                    tva = t_more(bld, sem="tt" if jj == 3 else None)
            if len(pe_pend) >= 2:
                emit_pe_transpose(pe_pend.pop(0))
            mw = [mx_free[sl]] if mx_free[sl][0] else []
            v_up = E.op("vector", lambda e, n=n, j4=j4, sl=sl: e.tensor_copy(
                mx4[:, sl, :, :], ap["numer"][:, n, 4 * j4:4 * j4 + 4, :]),
                sem="vv", waits=mw)
            v_add = E.op("vector", lambda e, sl=sl: e.tensor_tensor(
                ap["mx_buf"][:, sl, :], pbF[sl][:, 0:512],
                ap["mx_buf"][:, sl, :], ALU.add),
                sem="vv", waits=[("vv", v_up), ("tt", tva)])
            consume(sl, "vv", v_add)
            hw = [hspb_free[sl]] if hspb_free[sl][0] else []
            sc_marks = []
            for jj in range(4):
                i = n * H + j4 * 4 + jj
                dst = ap["hsp_bf"][:, sl, jj * P:(jj + 1) * P]
                srcm = mx4[:, sl, jj, :]
                if jj % 2 == 0:
                    sc = E.op("scalar", lambda e, dst=dst, srcm=srcm, i=i:
                              e.activation(dst, srcm, AF.Copy,
                                           scale=ap["n1"][:, i:i + 1]),
                              sem="ss",
                              waits=[("vv", v_add), ("vv", v_rzf)] + hw)
                    sc_marks.append(("ss", sc))
                else:
                    sc = E.op("vector", lambda e, dst=dst, srcm=srcm, i=i:
                              e.tensor_scalar(dst, srcm,
                                              ap["n1"][:, i:i + 1], None,
                                              ALU.mult),
                              sem="vv",
                              waits=[("vv", v_add), ("vv", v_rzf)] + hw)
                    sc_marks.append(("vv", sc))
                hw = []
            mx_free[sl] = sc_marks[-1]
            pe_pend.append((n, j4, sl, sc_marks))
        if n > 1:
            pf_tile(0, n - 2)
    while pe_pend:
        emit_pe_transpose(pe_pend.pop(0))
    pf_tile(0, NCH - 2)
    pf_tile(0, NCH - 1)
    pe_marks = E.marks()
    dump("outnT", ap["outnT"], pe_marks)

    if phases_upto == "Pe":
        finalize(nc, E)
        return nc, dump_outs

    for fb in (2, 3):
        sl = fb % 2
        wo23_lv[fb] = E.op("gpsimd", lambda e, fb=fb, sl=sl: e.dma_start(
            out=ap["wgbuf"][:, sl, :, :],
            in_=wo_r[:, :, fb * 512:(fb + 1) * 512]),
            sem=f"wo{fb}", inc=16, waits=[("tt", pe_marks["tt"])])
    for ct in range(NT):
        pf_tile(1, ct)
    pf_prolog(2)
    for ct in range(NT):
        pf_tile(2, ct)
    wg0_free = ("tt", E.val("tt"))
    pf_prolog(3)
    for ct in range(NT):
        pf_tile(3, ct)
    wg1_free = ("tt", E.val("tt"))
    pf_marks = E.marks()
    ebuf_free[0] = xst_free[0][0]
    ebuf_free[1] = xst_free[1][0]
    obuf_pf_free = [xst_free[2][0], xst_free[3][0]]

    if phases_upto == "Pf":
        finalize(nc, E)
        return nc, dump_outs

    warm(AF.Silu, ap["wsc"])
    wg_free = [wg0_free, wg1_free]
    wu_free = [("vv", pe_marks["vv"]), ("vv", pe_marks["vv"])]
    wd_freeA = [("tt", pf_marks["tt"]), ("tt", pf_marks["tt"])]
    wd_freeB = [("tt", pf_marks["tt"]), ("tt", pf_marks["tt"])]
    oslot = [ap["obuf"][:, 0, :], ap["obuf"][:, 1, :],
             ap["mx_buf"][:, 0, :], ap["mx_buf"][:, 1, :]]
    osem = ["xo0", "xo1", "xe0", "xe1"]
    obuf_free = [obuf_pf_free[0], obuf_pf_free[1],
                 dst_free[2] if dst_free[2][0] else mx_free[0],
                 dst_free[3] if dst_free[3][0] else mx_free[1]]
    E.wait("vector", [("ss", pe_marks["ss"])])
    hspbf_free = [("tt", pe_marks["tt"]), ("tt", pe_marks["tt"])]
    pend = []
    uglob = [0]
    hsp_state = {"ga": ("tt", pf_marks["tt"]), "gb": ("tt", pf_marks["tt"])}

    def emit_hsp_transpose(item):
        u, ib8_, ct_, v_sp_ = item
        usl = u % 2
        trb = 6 + (u % 2)
        for j in range(4):
            bld = (lambda e, j=j, trb=trb, usl=usl: e.transpose(
                pbB[trb][:, j * P:(j + 1) * P],
                ap["hsp_bf"][:, usl, j * P:(j + 1) * P], ap["ident"]))
            if j == 0:
                tvt = t_first(trb, bld, waits=[("vv", v_sp_)])
            else:
                tvt = t_more(bld, sem="tt" if j == 3 else None)
        hspbf_free[usl] = ("tt", tvt)
        s0 = (ib8_ % 4) * 4
        dstn = "hspT_a" if ib8_ < 4 else "hspT_b"
        dsts = ap[dstn][:, s0:s0 + 4, ct_ * P:(ct_ + 1) * P]
        srcs = pbB[trb].rearrange("p (j c) -> p j c", c=P)[:, 0:4, :]
        g = hsp_state["ga" if ib8_ < 4 else "gb"]
        s_cp = E.op("scalar", lambda e, dsts=dsts, srcs=srcs: e.copy(dsts, srcs),
                    sem="ss", waits=[("tt", tvt), g])
        consume(trb, "ss", s_cp)

    for grp in range(2):
        for ib8 in range(8):
            ibg = grp * 8 + ib8
            reg = ibg % 2
            iofs = ibg * 512
            wgw = [wg_free[reg]] if wg_free[reg][0] else []
            lg = E.op("gpsimd", lambda e, reg=reg, iofs=iofs: e.dma_start(
                out=ap["wgbuf"][:, reg, :, :],
                in_=wg_r[:, :, iofs:iofs + 512]),
                sem=f"wg{reg}", inc=16, waits=wgw)
            wuw = [wu_free[reg]] if wu_free[reg][0] else []
            lu = E.op("gpsimd", lambda e, reg=reg, iofs=iofs: e.dma_start(
                out=ap["wubuf"][:, reg, :, :],
                in_=wu_r[:, :, iofs:iofs + 512]),
                sem=f"wu{reg}", inc=16, waits=wuw)
            for ct in range(NT):
                csl = ct % 2
                gb = csl * 2
                ub = csl * 2 + 1
                for k in range(NF):
                    bld = (lambda e, k=k, reg=reg, ct=ct, gb=gb: e.matmul(
                        pbF[gb][:, 0:512],
                        ap["xnT"][:, k, ct * P:(ct + 1) * P],
                        ap["wgbuf"][:, reg, k, :],
                        start=(k == 0), stop=(k == NF - 1)))
                    if k == 0:
                        tvg = t_first(gb, bld, waits=[(f"wg{reg}", lg), ("ss", m_xn["ss"])])
                    else:
                        tvg = t_more(bld, sem="tt" if k == NF - 1 else None)
                for k in range(NF):
                    bld = (lambda e, k=k, reg=reg, ct=ct, ub=ub: e.matmul(
                        pbF[ub][:, 0:512],
                        ap["xnT"][:, k, ct * P:(ct + 1) * P],
                        ap["wubuf"][:, reg, k, :],
                        start=(k == 0), stop=(k == NF - 1)))
                    if k == 0:
                        tv = t_first(ub, bld, waits=[(f"wu{reg}", lu)])
                    else:
                        tv = t_more(bld, sem="tt" if k == NF - 1 else None)
                ew0 = [ebuf_free[0]] if ebuf_free[0][0] else []
                s_si = E.op("scalar", lambda e, gb=gb: e.activation(
                    ap["e_buf"][:, 0, :], pbF[gb][:, 0:512], AF.Silu),
                    sem="ss", waits=[("tt", tv)] + ew0)
                consume(gb, "ss", s_si)
                ew1 = [ebuf_free[1]] if ebuf_free[1][0] else []
                v_h = E.op("vector", lambda e, ub=ub: e.tensor_tensor(
                    ap["e_buf"][:, 1, :], ap["e_buf"][:, 0, :], pbF[ub][:, 0:512],
                    ALU.mult), sem="vv", waits=[("ss", s_si)] + ew1)
                ebuf_free[0] = ("vv", v_h)
                consume(ub, "vv", v_h)
                sq = ap["sqbuf"].rearrange("p (q f) -> p q f", f=4)
                gq = ap["geb"].rearrange("p (q f) -> p q f", f=4)
                E.op("vector", lambda e: e.tensor_mul(
                    ap["sqbuf"], ap["e_buf"][:, 1, :], ap["e_buf"][:, 1, :]),
                    waits=[("vv", v_h)])
                E.op("vector", lambda e, sq=sq: e.tensor_tensor(
                    ap["m1"], sq[:, :, 0], sq[:, :, 1], ALU.max))
                E.op("vector", lambda e, sq=sq: e.tensor_tensor(
                    ap["n1"], sq[:, :, 0], sq[:, :, 1], ALU.min))
                E.op("vector", lambda e, sq=sq: e.tensor_tensor(
                    ap["m2"], sq[:, :, 2], sq[:, :, 3], ALU.max))
                E.op("vector", lambda e, sq=sq: e.tensor_tensor(
                    ap["n2"], sq[:, :, 2], sq[:, :, 3], ALU.min))
                E.op("vector", lambda e: e.tensor_tensor(
                    ap["m1"], ap["m1"], ap["m2"], ALU.min))
                E.op("vector", lambda e: e.tensor_tensor(
                    ap["n1"], ap["n1"], ap["n2"], ALU.max))
                E.op("vector", lambda e: e.tensor_tensor(
                    ap["tq"], ap["m1"], ap["n1"], ALU.max))
                for j in range(4):
                    E.op("vector", lambda e, j=j, sq=sq, gq=gq: e.tensor_tensor(
                        gq[:, :, j], sq[:, :, j], ap["tq"], ALU.is_ge))
                usl = uglob[0] % 2
                hw_ = [hspbf_free[usl]] if hspbf_free[usl][0] else []
                v_sp = E.op("vector", lambda e, usl=usl: e.tensor_tensor(
                    ap["hsp_bf"][:, usl, :], ap["e_buf"][:, 1, :], ap["geb"],
                    ALU.mult), sem="vv", waits=hw_)
                ebuf_free[1] = ("vv", v_sp)
                pend.append((uglob[0], ib8, ct, v_sp))
                uglob[0] += 1
                if len(pend) > 1:
                    emit_hsp_transpose(pend.pop(0))
            wg_free[reg] = ("tt", E.val("tt"))
            wu_free[reg] = ("tt", E.val("tt"))
        while pend:
            emit_hsp_transpose(pend.pop(0))
        gu_marks = E.marks()
        if grp == 0:
            dump("hsa0", ap["hspT_a"], gu_marks)
            dump("hsb0", ap["hspT_b"], gu_marks)
        wd_lv = {}

        def emit_wd_loads(fb, extra, grp=grp):
            slw = fb % 2
            wA = ([wd_freeA[slw]] if wd_freeA[slw][0] else []) + [wu_free[slw]]
            lwA = E.op("gpsimd", lambda e, grp=grp, fb=fb, slw=slw: e.dma_start(
                out=ap["wdbuf"][:, slw, :, :],
                in_=wd_r[:, grp * 32: grp * 32 + 16,
                         fb * 512:(fb + 1) * 512]),
                sem=f"wd{slw}", inc=16, waits=wA + extra)
            wB = ([wd_freeB[slw]] if wd_freeB[slw][0] else []) + [wg_free[slw]]
            lwB = E.op("gpsimd", lambda e, grp=grp, fb=fb, slw=slw: e.dma_start(
                out=ap["wdbuf2"][:, slw, :, :],
                in_=wd_r[:, grp * 32 + 16: grp * 32 + 32,
                         fb * 512:(fb + 1) * 512]),
                sem=f"wd{2 + slw}", inc=16, waits=wB + extra)
            wd_lv[fb] = (slw, lwA, lwB)

        emit_wd_loads(0, [])
        emit_wd_loads(1, [])
        fb_end_tt = {}
        for fb in range(4):
            slw, lwA, lwB = wd_lv[fb]
            for ct in range(NT):
                u = fb * NT + ct
                bank = 2 + (u % 4)
                for s in range(32):
                    hsname = "hspT_a" if s < 16 else "hspT_b"
                    srow = s % 16
                    wbname = "wdbuf" if s < 16 else "wdbuf2"
                    bld = (lambda e, hsname=hsname, srow=srow, ct=ct,
                           bank=bank, slw=slw, wbname=wbname, s=s: e.matmul(
                               pbF[bank][:, 0:512],
                               ap[hsname][:, srow, ct * P:(ct + 1) * P],
                               ap[wbname][:, slw, srow, :],
                               start=(s == 0), stop=(s == 31)))
                    if s == 0:
                        tv = t_first(bank, bld,
                                waits=[(f"wd{slw}", lwA), ("ss", gu_marks["ss"])])
                    elif s == 16:
                        tv = t_more(bld, waits=[(f"wd{2 + slw}", lwB)])
                    else:
                        tv = t_more(bld, sem="tt" if s == 31 else None)
                osl = u % 4
                ow = [obuf_free[osl]] if obuf_free[osl][0] else []
                ol = E.op("sync", lambda e, ct=ct, fb=fb, osl=osl: e.dma_start(
                    out=oslot[osl],
                    in_=out_r[:, ct, fb * 512:(fb + 1) * 512]),
                    sem=osem[osl], inc=16,
                    waits=ow + [tile_store[(ct, fb)]])
                dsl = u % 2
                dw = [dst_free[dsl]] if dst_free[dsl][0] else []
                vad = E.op("vector", lambda e, bank=bank, dsl=dsl, osl=osl: e.tensor_tensor(
                    ap["dstage"][:, dsl, :], pbF[bank][:, 0:512],
                    oslot[osl], ALU.add),
                    sem="vv", waits=[("tt", tv), (osem[osl], ol)] + dw)
                consume(bank, "vv", vad)
                obuf_free[osl] = ("vv", vad)
                sv = E.op("sync", lambda e, ct=ct, fb=fb, dsl=dsl: e.dma_start(
                    out=out_r[:, ct, fb * 512:(fb + 1) * 512],
                    in_=ap["dstage"][:, dsl, :]),
                    sem=f"sb{dsl}", inc=16, waits=[("vv", vad)])
                dst_free[dsl] = (f"sb{dsl}", sv)
                tile_store[(ct, fb)] = (f"sb{dsl}", sv)
            fb_end_tt[fb] = E.val("tt")
            wd_freeA[slw] = ("tt", fb_end_tt[fb])
            wd_freeB[slw] = ("tt", fb_end_tt[fb])
            if fb + 2 < 4:
                emit_wd_loads(fb + 2, [])
        hsp_state["ga"] = ("tt", E.val("tt"))
        hsp_state["gb"] = ("tt", E.val("tt"))
        wu_free[0] = ("tt", fb_end_tt[2])
        wu_free[1] = ("tt", fb_end_tt[3])
        wg_free[0] = ("tt", fb_end_tt[2])
        wg_free[1] = ("tt", fb_end_tt[3])

    finalize(nc, E)
    return nc, dump_outs


def finalize(nc, E):
    E.wait("sync", [("sb0", E.val("sb0")), ("sb1", E.val("sb1")),
                    ("sb2", E.val("sb2")), ("sb3", E.val("sb3")),
                    ("sd", E.val("sd"))])
    with contextlib.ExitStack() as ctx:
        for s in list(E.cnt):
            E.handles[s] = ctx.enter_context(nc.semaphore(s))
        block = ctx.enter_context(nc.Block())

        def run(name):
            def body(eng):
                for f in E.ops[name]:
                    f(eng)
            return body

        block.sync(run("sync"))
        block.gpsimd(run("gpsimd"))
        block.tensor(run("tensor"))
        block.vector(run("vector"))
        block.scalar(run("scalar"))


def kernel(**inputs):
    x = np.ascontiguousarray(np.asarray(inputs["L0_attn_x"], dtype=np.float32))
    Wq = np.ascontiguousarray(np.asarray(inputs["L0_attn_Wq"], dtype=np.float32))
    Wk = np.ascontiguousarray(np.asarray(inputs["L0_attn_Wk"], dtype=np.float32))
    Wv = np.ascontiguousarray(np.asarray(inputs["L0_attn_Wv"], dtype=np.float32))
    Wo = np.ascontiguousarray(np.asarray(inputs["L0_attn_Wo"], dtype=np.float32))
    bq = np.ascontiguousarray(np.asarray(inputs["L0_attn_bq"], dtype=np.float32)).reshape(1, DM)
    bk = np.ascontiguousarray(np.asarray(inputs["L0_attn_bk"], dtype=np.float32)).reshape(1, DM)
    bv = np.ascontiguousarray(np.asarray(inputs["L0_attn_bv"], dtype=np.float32)).reshape(1, DM)
    bo = np.ascontiguousarray(np.asarray(inputs["L0_attn_bo"], dtype=np.float32)).reshape(1, DM)
    wg = np.ascontiguousarray(np.asarray(inputs["L0_ffn_w_gate"], dtype=np.float32))
    wu = np.ascontiguousarray(np.asarray(inputs["L0_ffn_w_up"], dtype=np.float32))
    wd = np.ascontiguousarray(np.asarray(inputs["L0_ffn_w_down"], dtype=np.float32))

    R = 1024
    nc, _ = build_program(R=R)
    in_maps = []
    for c in range(8):
        s, hf = c // 2, c % 2
        fl = np.empty((P, 2), np.float32)
        fl[:, 0] = float(hf)
        fl[:, 1] = float(1 - hf)
        in_maps.append({
            "x": np.ascontiguousarray(x[s, hf * R:(hf + 1) * R]),
            "flag": fl,
            "wq": Wq, "wk": Wk, "wv": Wv, "wo": Wo,
            "bq": bq, "bk": bk, "bv": bv, "bo": bo,
            "wg": wg, "wu": wu, "wd": wd,
        })
    res = run_bass_kernel_spmd(nc, in_maps, core_ids=list(range(8)))
    out = np.empty((4, 2048, DM), np.float32)
    for c in range(8):
        s, hf = c // 2, c % 2
        out[s, hf * R:(hf + 1) * R] = res.results[c]["out"]
    return out


# version 21 (final)
# speedup vs baseline: 1.0091x; 1.0091x over previous
import contextlib

import numpy as np
import ml_dtypes

import concourse.bass as bass
import concourse.mybir as mybir
from concourse.bass_utils import run_bass_kernel_spmd

F32 = mybir.dt.float32
BF16 = mybir.dt.bfloat16
AF = mybir.ActivationFunctionType
ALU = mybir.AluOpType
AX = mybir.AxisListType
BFNP = ml_dtypes.bfloat16

P = 128
DM = 2048
NF = DM // P
H = 16
D = 128
FI = 8192
EPS_LN = 1e-5
EPS_AT = 1e-6

ENGINES = ("sync", "gpsimd", "tensor", "vector", "scalar")



class Emit:

    def __init__(self):
        self.ops = {e: [] for e in ENGINES}
        self.cnt = {}
        self.handles = {}
        self.waited = {e: {} for e in ENGINES}

    def sem(self, name):
        self.cnt[name] = 0

    def val(self, name):
        return self.cnt[name]

    def marks(self):
        return {s: self.cnt[s] for s in ("tt", "vv", "ss")}

    def _mkwait(self, eng, waits):
        real = []
        for s, v in waits:
            if s is None or v is None or v <= 0:
                continue
            if self.waited[eng].get(s, -1) >= v:
                continue
            self.waited[eng][s] = v
            real.append((s, v))
        return real

    def op(self, eng, build, sem=None, inc=1, waits=()):
        real = self._mkwait(eng, waits)
        v = None
        if sem is not None:
            self.cnt[sem] += inc
            v = self.cnt[sem]
        hs = self.handles

        def f(e, build=build, real=real, sem=sem, inc=inc):
            for s, vv in real:
                e.wait_ge(hs[s], vv)
            ins = build(e)
            if sem is not None:
                ins.then_inc(hs[sem], inc)

        self.ops[eng].append(f)
        return v

    def wait(self, eng, waits):
        real = self._mkwait(eng, waits)
        hs = self.handles
        if not real:
            return

        def f(e, real=real):
            for s, vv in real:
                e.wait_ge(hs[s], vv)

        self.ops[eng].append(f)


def build_program(R=1024, dumps=(), phases_upto="P5"):
    NT = R // P
    NCH = R // P
    TB = 512 if R >= 512 else R
    NTB = max(R // TB, 1)

    nc = bass.Bass("TRN2", target_bir_lowering=False, debug=False,
                   detect_race_conditions=False)

    x_h = nc.dram_tensor("x", [R, DM], F32, kind="ExternalInput").ap()
    flag_h = nc.dram_tensor("flag", [P, 2], F32, kind="ExternalInput").ap()
    wq_h = nc.dram_tensor("wq", [DM, DM], F32, kind="ExternalInput").ap()
    wk_h = nc.dram_tensor("wk", [DM, DM], F32, kind="ExternalInput").ap()
    wv_h = nc.dram_tensor("wv", [DM, DM], F32, kind="ExternalInput").ap()
    wo_h = nc.dram_tensor("wo", [DM, DM], F32, kind="ExternalInput").ap()
    bq_h = nc.dram_tensor("bq", [1, DM], F32, kind="ExternalInput").ap()
    bk_h = nc.dram_tensor("bk", [1, DM], F32, kind="ExternalInput").ap()
    bv_h = nc.dram_tensor("bv", [1, DM], F32, kind="ExternalInput").ap()
    bo_h = nc.dram_tensor("bo", [1, DM], F32, kind="ExternalInput").ap()
    wg_h = nc.dram_tensor("wg", [DM, FI], F32, kind="ExternalInput").ap()
    wu_h = nc.dram_tensor("wu", [DM, FI], F32, kind="ExternalInput").ap()
    wd_h = nc.dram_tensor("wd", [FI, DM], F32, kind="ExternalInput").ap()
    out_h = nc.dram_tensor("out", [R, DM], F32, kind="ExternalOutput").ap()
    cc_in_h = nc.dram_tensor("cc_in", [P, H * (D + 1)], BF16).ap()
    cc_out_h = nc.dram_tensor("cc_out", [P, H * (D + 1)], BF16).ap()

    x_r = x_h.rearrange("(t p) f -> p t f", p=P)
    out_r = out_h.rearrange("(t p) f -> p t f", p=P)
    wq_r = wq_h.rearrange("(k p) n -> p k n", p=P)
    wk_r = wk_h.rearrange("(k p) n -> p k n", p=P)
    wv_r = wv_h.rearrange("(k p) n -> p k n", p=P)
    wo_r = wo_h.rearrange("(k p) n -> p k n", p=P)
    wg_r = wg_h.rearrange("(k p) n -> p k n", p=P)
    wu_r = wu_h.rearrange("(k p) n -> p k n", p=P)
    wd_r = wd_h.rearrange("(s p) n -> p s n", p=P)

    ident_c = nc.inline_tensor(np.eye(P, dtype=BFNP), name="identc").ap()
    identf_c = nc.inline_tensor(np.eye(16, dtype=np.float32), name="identfc").ap()
    mask_c = nc.inline_tensor(np.triu(np.ones((P, P), dtype=np.float32)),
                              name="maskc").ap()

    A = {}

    BASE0 = (nc.sbuf_base + 31) & ~31
    assert BASE0 + 211712 <= nc.sbuf_top, (BASE0, nc.sbuf_top)

    def sb(name, off, shape, dtype):
        A[name] = nc.alloc_sbuf_tensor_at(name, shape, dtype, offset=BASE0 + off).ap()

    sb("ident", 0, [P, P], BF16)
    sb("mask", 256, [P, P], F32)
    sb("bqt", 768, [P, NF], F32)
    sb("bkt", 832, [P, NF], F32)
    sb("bvt", 896, [P, NF], F32)
    sb("flag", 960, [P, 2], F32)
    sb("ones_row", 992, [1, 512], F32)
    sb("b_row", 3040, [1, DM], F32)
    sb("rep", 11232, [P, 512], F32)
    sb("b16", 11232, [16, P], F32)
    sb("e_buf", 13280, [P, 2, 512], F32)
    S_OFF = 17376
    sb("S", S_OFF, [P, H, D + 1], F32)
    sb("sqbuf", S_OFF, [P, 512], F32)
    sb("geb", S_OFF + 2048, [P, 512], F32)
    sb("hsp_bf", S_OFF + 4096, [P, 2, 512], BF16)
    sb("S_bf", 25632, [P, H, D + 1], BF16)
    sb("kf_tok", 29760, [P, 4, P], BF16)
    sb("identf", 30784, [16, 16], F32)
    sb("sum_t", 31360, [P, 8], F32)
    sb("sumsq_t", 31392, [P, 8], F32)
    sb("m_t", 31424, [P, 8], F32)
    sb("v_t", 31456, [P, 8], F32)
    sb("rstd_t", 31488, [P, 8], F32)
    sb("z_sb", 31552, [P, NCH * H], F32)
    sb("wsc", 32064, [1, 64], F32)
    R1 = 32768
    sb("x_stage", R1, [P, 2, DM], F32)
    sb("xn_stage", R1 + 16384, [P, DM], BF16)
    sb("trash", R1 + 20480, [P, DM], BF16)
    sb("wtile", R1 + 16384, [P, 4, NF, P], BF16)
    sb("wvbuf", R1, [P, 2, NF, 512], BF16)
    sb("numer", R1, [P, NCH, H, P], BF16)
    sb("wubuf", R1, [P, 2, NF, 512], BF16)
    sb("wdbuf", R1, [P, 2, 16, 512], BF16)
    R2 = R1 + 32768
    sb("xnT", R2, [P, NF, R], BF16)
    R3 = R2 + 32768
    sb("qfT", R3, [P, NF, R], BF16)
    sb("xfullA", R3, [P, 4, DM], F32)
    sb("wgbuf", R3, [P, 2, NF, 512], BF16)
    sb("wdbuf2", R3, [P, 2, 16, 512], BF16)
    R4 = R3 + 32768
    sb("kfT", R4, [P, NF, R], BF16)
    sb("xfullB", R4, [P, 4, DM], F32)
    sb("outnT", R4, [P, NF, R], BF16)
    sb("hspT_b", R4, [P, 16, R], BF16)
    R5 = R4 + 32768
    sb("vaug", R5, [P, NCH, H, D + 1], BF16)
    sb("wobuf", R5, [P, 2, NF, 512], BF16)
    sb("hspT_a", R5, [P, 16, R], BF16)
    EX = R5 + NCH * H * (D + 1) * 2
    sb("dstage", EX, [P, 2, 512], F32)
    sb("obuf", EX + 4096, [P, 2, 512], F32)
    sb("m1", EX + 8192, [P, P], F32)
    sb("n1", EX + 8704, [P, P], F32)
    sb("at_m", EX + 8192, [P, 4, P], BF16)
    sb("m2", EX + 9216, [P, P], F32)
    sb("n2", EX + 9728, [P, P], F32)
    sb("tq", EX + 10240, [P, P], F32)
    sb("mx_buf", EX + 10752, [P, 2, 512], F32)

    pb = [nc.alloc_psum_tensor(f"pb{i}", [P, 512], F32) for i in range(8)]
    pbF = [t.ap() for t in pb]
    pbB = [t.ap().bitcast(BF16) for t in pb]

    ap = A
    E = Emit()
    for s in ("tt", "vv", "ss", "xc", "xb", "xs0", "xs1", "xs2",
              "sb2", "sb3", "xe0", "xe1",
              "xo0", "xo1", "wt0", "wt1", "wt2", "wt3",
              "wv0", "wv1", "wv2", "wv3",
              "wo0", "wo1", "wo2", "wo3",
              "wg0", "wg1", "wu0", "wu1", "wd0", "wd1", "wd2", "wd3",
              "sb0", "sb1", "sd",
              "cci", "ccs", "cco"):
        E.sem(s)

    bank_waits = [[] for _ in range(8)]

    def t_first(bank, build, waits=(), sem=None):
        w = list(waits) + bank_waits[bank]
        bank_waits[bank] = []
        return E.op("tensor", build, sem=sem, waits=w)

    def t_more(build, waits=(), sem=None):
        return E.op("tensor", build, sem=sem, waits=waits)

    def consume(bank, sem, v):
        bank_waits[bank].append((sem, v))

    dump_outs = {}

    def dump(name, src_ap, marks):
        if name not in dumps:
            return
        dh = nc.dram_tensor("dump_" + name, list(src_ap.shape),
                            src_ap.dtype, kind="ExternalOutput").ap()
        dump_outs[name] = dh
        E.op("sync", lambda e: e.dma_start(out=dh, in_=src_ap),
             sem="sd", inc=16,
             waits=[(k, v) for k, v in marks.items() if k in ("tt", "vv", "ss")])

    E.op("sync", lambda e: e.dma_start(out=ap["ident"], in_=ident_c), sem="xc", inc=16)
    E.op("sync", lambda e: e.dma_start(out=ap["mask"], in_=mask_c), sem="xc", inc=16)
    E.op("sync", lambda e: e.dma_start(out=ap["flag"], in_=flag_h), sem="xc", inc=16)
    E.op("sync", lambda e: e.dma_start(out=ap["identf"], in_=identf_c), sem="xc", inc=16)
    consts_xl = E.val("xc")
    E.op("vector", lambda e: e.memset(ap["ones_row"], 1.0), sem="vv")
    E.op("vector", lambda e: e.memset(ap["S"][:, :, :], 0.0), sem="vv")
    base_vv = E.val("vv")

    def build_bias_tile(bh, dst):
        lv = E.op("sync", lambda e, bh=bh: e.dma_start(
            out=ap["b16"], in_=bh.rearrange("o (h c) -> (o h) c", c=P)),
            sem="xb", inc=16, waits=[("tt", E.val("tt"))])
        tv = t_first(0, lambda e: e.transpose(
            pbF[0][:, 0:H], ap["b16"], ap["identf"]),
            waits=[("xb", lv), ("xc", consts_xl)], sem="tt")
        vv = E.op("vector", lambda e, dst=dst: e.tensor_copy(ap[dst], pbF[0][:, 0:H]),
                  sem="vv", waits=[("tt", tv)])
        consume(0, "vv", vv)

    build_bias_tile(bq_h, "bqt")
    build_bias_tile(bk_h, "bkt")
    build_bias_tile(bv_h, "bvt")

    brow = {"xl": None}

    def load_brow(bh, waits=None):
        if waits is None:
            waits = [("tt", E.val("tt"))]
        brow["xl"] = E.op("sync", lambda e, bh=bh: e.dma_start(out=ap["b_row"], in_=bh),
                          sem="xb", inc=16, waits=waits)

    load_brow(bv_h)

    def build_rep(blk):
        tv = t_first(5, lambda e, blk=blk: e.matmul(
            pbF[5][:, 0:512], ap["ones_row"][0:1, 0:P],
            ap["b_row"][0:1, blk * 512:(blk + 1) * 512], start=True, stop=True),
            waits=[("xb", brow["xl"]), ("vv", E.val("vv"))], sem="tt")
        vv = E.op("vector", lambda e: e.tensor_copy(ap["rep"], pbF[5][:, 0:512]),
                  sem="vv", waits=[("tt", tv)])
        consume(5, "vv", vv)
        return vv

    def warm(func, tgt):
        E.op("scalar", lambda e, func=func: e.preload_act_table(func))
        E.op("scalar", lambda e, func=func, tgt=tgt: e.activation(
            tgt, ap["ones_row"][0:1, 0:64], func), sem="ss",
            waits=[("vv", base_vv)])

    ln_state = {
        "xfree": [(None, 0), (None, 0)],
        "tile_vv": [],
    }

    def ln_phase(xv):
        warm(AF.Square, ap["wsc"])
        last_ss = 0
        last_vv = 0
        xs_val = []
        qmap = [("sync", "xs0"), ("scalar", "xs1")]
        for t in range(NT):
            st = ap["xfullA"][:, t, :] if t < 4 else ap["xfullB"][:, t - 4, :]
            qe, qs = qmap[t % 2]
            xl = E.op(qe, lambda e, st=st, t=t: e.dma_start(
                out=st, in_=xv[:, t, :]), sem=qs, inc=16)
            xs_val.append((qs, xl))
        for t in range(NT):
            st = ap["xfullA"][:, t, :] if t < 4 else ap["xfullB"][:, t - 4, :]
            last_ss = E.op("scalar", lambda e, st=st, t=t: e.activation(
                ap["trash"], st, AF.Square, accum_out=ap["sumsq_t"][:, t:t + 1]),
                sem="ss", waits=[xs_val[t]])
            last_vv = E.op("vector", lambda e, st=st, t=t: e.tensor_reduce(
                ap["sum_t"][:, t:t + 1], st, AX.X, ALU.add),
                sem="vv", waits=[xs_val[t]])
        v1 = E.op("vector", lambda e: e.tensor_scalar(
            ap["m_t"][:, 0:NT], ap["sum_t"][:, 0:NT], 1.0 / DM, None, ALU.mult),
            sem="vv", waits=[("vv", last_vv), ("ss", last_ss)])
        v2 = E.op("vector", lambda e: e.tensor_scalar(
            ap["v_t"][:, 0:NT], ap["sumsq_t"][:, 0:NT], 1.0 / DM, None, ALU.mult),
            sem="vv", waits=[("vv", v1)])
        v3 = E.op("vector", lambda e: e.tensor_tensor(
            ap["rstd_t"][:, 0:NT], ap["m_t"][:, 0:NT], ap["m_t"][:, 0:NT],
            ALU.mult), sem="vv", waits=[("vv", v2)])
        v4 = E.op("vector", lambda e: e.tensor_tensor(
            ap["v_t"][:, 0:NT], ap["v_t"][:, 0:NT], ap["rstd_t"][:, 0:NT],
            ALU.subtract), sem="vv", waits=[("vv", v3)])
        v5 = E.op("vector", lambda e: e.tensor_scalar(
            ap["v_t"][:, 0:NT], ap["v_t"][:, 0:NT], EPS_LN, None, ALU.add),
            sem="vv", waits=[("vv", v4)])
        warm(AF.Sqrt, ap["wsc"])
        sq = E.op("scalar", lambda e: e.activation(
            ap["rstd_t"][:, 0:NT], ap["v_t"][:, 0:NT], AF.Sqrt),
            sem="ss", waits=[("vv", v5)])
        rv = E.op("vector", lambda e: e.reciprocal(
            ap["rstd_t"][:, 0:NT], ap["rstd_t"][:, 0:NT]),
            sem="vv", waits=[("ss", sq)])
        tile_tt = []
        for t in range(NT):
            st = ap["xfullA"][:, t, :] if t < 4 else ap["xfullB"][:, t - 4, :]
            xnname = "xn_stage" if t % 2 == 0 else "trash"
            xw = [("vv", rv)] + ([tile_tt[t - 2]] if t >= 2 else [])
            xnv = E.op("vector", lambda e, st=st, t=t, xnname=xnname: e.tensor_scalar(
                ap[xnname], st, ap["m_t"][:, t:t + 1],
                ap["rstd_t"][:, t:t + 1], ALU.subtract, ALU.mult),
                sem="vv", waits=xw)
            for half in range(2):
                bank = 6 + half
                for j in range(8):
                    k = half * 8 + j
                    bld = (lambda e, k=k, j=j, bank=bank, xnname=xnname: e.transpose(
                        pbB[bank][:, j * P:(j + 1) * P],
                        ap[xnname][:, k * P:(k + 1) * P], ap["ident"]))
                    if j == 0:
                        tv = t_first(bank, bld, waits=[("vv", xnv), ("xc", consts_xl)])
                    else:
                        tv = t_more(bld, sem="tt" if j == 7 else None)
                dst = ap["xnT"][:, half * 8:(half + 1) * 8, t * P:(t + 1) * P]
                src = pbB[bank].rearrange("p (j c) -> p j c", c=P)[:, 0:8, :]
                sv = E.op("scalar", lambda e, dst=dst, src=src: e.copy(dst, src),
                          sem="ss", waits=[("tt", tv)])
                consume(bank, "ss", sv)
            tile_tt.append(("tt", tv))
            ln_state["tile_vv"].append(("ss", sv))

    wtile_free = [(None, 0)] * 4
    wtile_order = [2, 3, 0, 1]
    ebuf_free = [(None, 0), (None, 0)]

    def proj_feat(w_r_view, bias, dst, xn_marks):
        evac = {}
        tiles = ln_state["tile_vv"]
        for m in range(NF):
            sl = wtile_order[m % 4]
            w = [wtile_free[sl]] if wtile_free[sl][0] else []
            if sl < 2 and wtile_free[sl][0] is None:
                w = [tiles[-1]]
            lva = E.op("gpsimd", lambda e, m=m, sl=sl: e.dma_start(
                out=ap["wtile"][:, sl, :, :],
                in_=w_r_view[:, :, m * P:(m + 1) * P]),
                sem=f"wt{sl}", inc=16, waits=w)
            for tb in range(NTB):
                u = m * NTB + tb
                bank = u % 4
                w2 = [(f"wt{sl}", lva), tiles[min(tb * 4 + 3, len(tiles) - 1)]]
                for k in range(NF):
                    bld = (lambda e, k=k, sl=sl, tb=tb, bank=bank: e.matmul(
                        pbF[bank][:, 0:TB], ap["wtile"][:, sl, k, :],
                        ap["xnT"][:, k, tb * TB:(tb + 1) * TB],
                        start=(k == 0), stop=(k == NF - 1)))
                    if k == 0:
                        tv = t_first(bank, bld, waits=w2)
                    else:
                        tv = t_more(bld, sem="tt" if k == NF - 1 else None)
                esl = u % 2
                dslice = ap[dst][:, m, tb * TB:(tb + 1) * TB]
                mw = [("ss", evac[u - 2][0][1])] if (u - 2) in evac else []
                v1 = E.op("vector", lambda e, bank=bank, esl=esl, m=m: e.tensor_scalar(
                    ap["mx_buf"][:, esl, 0:TB], pbF[bank][:, 0:TB],
                    ap[bias][:, m:m + 1], None, ALU.add),
                    sem="vv", waits=[("tt", tv)] + mw)
                ew = [ebuf_free[esl]] if ebuf_free[esl][0] else []
                s1 = E.op("scalar", lambda e, esl=esl: e.activation(
                    ap["e_buf"][:, esl, 0:TB], ap["mx_buf"][:, esl, 0:TB], AF.Exp),
                    sem="ss", waits=[("vv", v1)] + ew)
                v2a = E.op("vector", lambda e, esl=esl: e.tensor_scalar(
                    ap["mx_buf"][:, esl, 0:TB], ap["mx_buf"][:, esl, 0:TB],
                    0.0, None, ALU.max), waits=[("ss", s1)])
                v2 = E.op("vector", lambda e, esl=esl, dslice=dslice:
                          e.scalar_tensor_tensor(
                              dslice, ap["e_buf"][:, esl, 0:TB], 1.0,
                              ap["mx_buf"][:, esl, 0:TB], ALU.min, ALU.add),
                          sem="vv")
                evac[u] = [("ss", s1), ("vv", v2)]
                ebuf_free[esl] = ("vv", v2)
                consume(bank, "vv", v1)
            wtile_free[sl] = ("tt", E.val("tt"))
        return E.marks()

    wv_free = [(None, 0), (None, 0)]

    def v_pass(xn_marks, vaug_guard, r1_guards):
        ones_w = ([vaug_guard] if vaug_guard[0] else [])
        ov = E.op("vector", lambda e: e.memset(ap["vaug"][:, :, :, D:D + 1], 1.0),
                  sem="vv", waits=ones_w)
        tiles = ln_state["tile_vv"]
        for blk in range(4):
            sl = blk % 2
            w = ([wv_free[sl]] if wv_free[sl][0] else []) + list(r1_guards[sl])
            lva = E.op("gpsimd", lambda e, blk=blk, sl=sl: e.dma_start(
                out=ap["wvbuf"][:, sl, :, :],
                in_=wv_r[:, :, blk * 512:(blk + 1) * 512]),
                sem=f"wv{sl}", inc=16, waits=w)
            rep_vv = build_rep(blk)
            for ct in range(NT):
                u = blk * NT + ct
                bank = u % 4
                w2 = [(f"wv{sl}", lva), tiles[min(ct, len(tiles) - 1)]]
                for k in range(NF):
                    bld = (lambda e, k=k, sl=sl, ct=ct, bank=bank: e.matmul(
                        pbF[bank][:, 0:512], ap["xnT"][:, k, ct * P:(ct + 1) * P],
                        ap["wvbuf"][:, sl, k, :],
                        start=(k == 0), stop=(k == NF - 1)))
                    if k == 0:
                        tv = t_first(bank, bld, waits=w2)
                    else:
                        tv = t_more(bld, sem="tt" if k == NF - 1 else None)
                last_v = None
                for j in range(4):
                    h = 4 * blk + j
                    last_v = E.op("vector", lambda e, bank=bank, j=j, ct=ct, h=h:
                                  e.tensor_tensor(
                                      ap["vaug"][:, ct, h, 0:D],
                                      pbF[bank][:, j * P:(j + 1) * P],
                                      ap["rep"][:, j * P:(j + 1) * P], ALU.add),
                                  sem="vv",
                                  waits=[("tt", tv), ("vv", rep_vv), ("vv", ov)])
                consume(bank, "vv", last_v)
            wv_free[sl] = ("tt", E.val("tt"))
        return E.marks()

    ln_phase(x_r)
    m_xn = E.marks()
    dump("xnT", ap["xnT"], m_xn)
    if phases_upto == "Pa":
        finalize(nc, E)
        return nc, dump_outs

    warm(AF.Exp, ap["wsc"])
    m_q = proj_feat(wq_r, "bqt", "qfT", m_xn)
    dump("qfT", ap["qfT"], m_q)
    m_k = proj_feat(wk_r, "bkt", "kfT", m_xn)
    dump("kfT", ap["kfT"], m_k)
    m_v = v_pass(m_k, (None, 0),
                 ([("vv", m_xn["vv"])],
                  [("tt", m_k["tt"]), ("ss", m_k["ss"]), ("vv", m_k["vv"])]))
    pb_marks = E.marks()
    dump("vaug", ap["vaug"], pb_marks)

    if phases_upto == "Pb":
        finalize(nc, E)
        return nc, dump_outs

    E.wait("vector", [("tt", pb_marks["tt"]), ("ss", pb_marks["ss"])])
    sbf_val = {}
    pc_state = {}
    kft_free = [(None, 0)] * 4
    atm_free = [(None, 0)] * 4
    state_q = []

    def emit_state(item):
        j, n, h, s4, p2, vc, v2 = item
        if n > 0:
            t_first(p2, lambda e, n=n, h=h, p2=p2: e.matmul(
                pbF[p2][:, 0:D + 1], ap["qfT"][:, h, n * P:(n + 1) * P],
                ap["S_bf"][:, h, :], start=True, stop=False),
                waits=[sbf_val[h]])
            tv4 = t_more(lambda e, n=n, h=h, s4=s4, p2=p2: e.matmul(
                pbF[p2][:, 0:D + 1], ap["at_m"][:, s4, :],
                ap["vaug"][:, n, h, :], start=False, stop=True),
                waits=[("vv", v2)], sem="tt")
        else:
            tv4 = t_first(p2, lambda e, n=n, h=h, s4=s4, p2=p2: e.matmul(
                pbF[p2][:, 0:D + 1], ap["at_m"][:, s4, :],
                ap["vaug"][:, n, h, :], start=True, stop=True),
                waits=[("vv", v2)], sem="tt")
        sbank = 2 if p2 == 0 else 7
        tv5 = t_first(sbank, lambda e, s4=s4, n=n, h=h, sbank=sbank: e.matmul(
            pbF[sbank][:, 0:D + 1], ap["kf_tok"][:, s4, :],
            ap["vaug"][:, n, h, :], start=True, stop=True),
            waits=[("ss", vc)], sem="tt")
        v_num = E.op("vector", lambda e, p2=p2, n=n, h=h: e.tensor_copy(
            ap["numer"][:, n, h, :], pbF[p2][:, 0:D]),
            sem="vv", waits=[("tt", tv4)])
        s_zev = E.op("scalar", lambda e, p2=p2, i=n * H + h: e.copy(
            ap["z_sb"][:, i:i + 1], pbF[p2][:, D:D + 1]),
            sem="ss", waits=[("tt", tv4), ("vv", v_num)])
        consume(p2, "vv", v_num)
        consume(p2, "ss", s_zev)
        atm_free[s4] = ("tt", tv4)
        va = E.op("vector", lambda e, h=h, sbank=sbank: e.tensor_tensor(
            ap["S"][:, h, :], ap["S"][:, h, :], pbF[sbank][:, 0:D + 1], ALU.add),
            sem="vv", waits=[("tt", tv5)])
        consume(sbank, "vv", va)
        kft_free[s4] = ("tt", tv5)
        if n < NCH - 1:
            vb = E.op("vector", lambda e, h=h: e.tensor_copy(
                ap["S_bf"][:, h, :], ap["S"][:, h, :]), sem="vv",
                waits=[("vv", va)])
            sbf_val[h] = ("vv", vb)
        if n == 3 and h == H - 1:
            pc_state["vaug03_tt"] = E.val("tt")

    for n in range(NCH):
        for h in range(H):
            i = n * H + h
            s4 = i % 4
            p2 = i % 2
            if len(state_q) >= 4:
                emit_state(state_q.pop(0))
            tb_bank = 5 + p2
            tv1 = t_first(tb_bank, lambda e, n=n, h=h, tb_bank=tb_bank: e.transpose(
                pbB[tb_bank][:, 0:P], ap["kfT"][:, h, n * P:(n + 1) * P],
                ap["ident"]), waits=[("vv", pb_marks["vv"])], sem="tt")
            kw = [kft_free[s4]] if kft_free[s4][0] else []
            vc = E.op("scalar", lambda e, s4=s4, tb_bank=tb_bank: e.copy(
                ap["kf_tok"][:, s4, :], pbB[tb_bank][:, 0:P]),
                sem="ss", waits=[("tt", tv1)] + kw)
            consume(tb_bank, "ss", vc)
            ab = 3 + p2
            tv2 = t_first(ab, lambda e, n=n, h=h, ab=ab: e.matmul(
                pbF[ab][:, 0:P], ap["kfT"][:, h, n * P:(n + 1) * P],
                ap["qfT"][:, h, n * P:(n + 1) * P], start=True, stop=True), sem="tt")
            aw = [atm_free[s4]] if atm_free[s4][0] else []
            v2 = E.op("vector", lambda e, s4=s4, ab=ab: e.tensor_tensor(
                ap["at_m"][:, s4, :], pbF[ab][:, 0:P], ap["mask"], ALU.mult),
                sem="vv", waits=[("tt", tv2)] + aw)
            consume(ab, "vv", v2)
            state_q.append((i, n, h, s4, p2, vc, v2))
    while state_q:
        emit_state(state_q.pop(0))
    pc_marks = E.marks()
    dump("numer", ap["numer"], pc_marks)
    dump("Sloc", ap["S"], pc_marks)

    if phases_upto == "Pc":
        finalize(nc, E)
        return nc, dump_outs

    wo_lv = {}
    wo_lv[0] = E.op("gpsimd", lambda e: e.dma_start(
        out=ap["wobuf"][:, 0, :, :], in_=wo_r[:, :, 0:512]),
        sem="wo0", inc=16, waits=[("tt", pc_state["vaug03_tt"])])
    S_flat = ap["S"].rearrange("p h d -> p (h d)")
    Sbf_flat = ap["S_bf"].rearrange("p h d -> p (h d)")
    vfl = E.op("vector", lambda e: e.tensor_scalar(
        Sbf_flat, S_flat, ap["flag"][:, 1:2], None, ALU.mult),
        sem="vv", waits=[("xc", consts_xl), ("vv", pc_marks["vv"])])
    cl = E.op("sync", lambda e: e.dma_start(out=cc_in_h, in_=Sbf_flat),
              sem="cci", inc=16, waits=[("vv", vfl)])
    E.op("gpsimd", lambda e: e.collective_compute(
        "AllReduce", ALU.add,
        replica_groups=[[0, 1], [2, 3], [4, 5], [6, 7]],
        ins=[cc_in_h.opt()], outs=[cc_out_h.opt()]),
        sem="ccs", inc=1, waits=[("cci", cl)])
    co = E.op("sync", lambda e: e.dma_start(out=Sbf_flat, in_=cc_out_h),
              sem="cco", inc=16, waits=[("ccs", 1)])
    wo_lv[1] = E.op("gpsimd", lambda e: e.dma_start(
        out=ap["wobuf"][:, 1, :, :], in_=wo_r[:, :, 512:1024]),
        sem="wo1", inc=16, waits=[("tt", pc_marks["tt"])])
    sbf_pre = E.op("vector", lambda e: e.tensor_scalar(
        Sbf_flat, Sbf_flat, ap["flag"][:, 0:1], None, ALU.mult),
        sem="vv", waits=[("cco", co)])
    dump("Spre", ap["S_bf"], E.marks())

    load_brow(bo_h, waits=[("tt", pb_marks["tt"])])

    def build_rep_pf(blk, dst, extra):
        tv = t_first(5, lambda e, blk=blk: e.matmul(
            pbF[5][:, 0:512], ap["ones_row"][0:1, 0:P],
            ap["b_row"][0:1, blk * 512:(blk + 1) * 512], start=True, stop=True),
            waits=[("xb", brow["xl"])], sem="tt")
        vv = E.op("vector", lambda e, dst=dst: e.tensor_copy(ap[dst], pbF[5][:, 0:512]),
                  sem="vv", waits=[("tt", tv)] + extra)
        consume(5, "vv", vv)
        return vv

    wo_free = [(None, 0), (None, 0)]
    xslot = [ap["e_buf"][:, 0, :], ap["e_buf"][:, 1, :],
             ap["obuf"][:, 0, :], ap["obuf"][:, 1, :]]
    xsem = ["xe0", "xe1", "xo0", "xo1"]
    xst_free = [[ebuf_free[0]] if ebuf_free[0][0] else [],
                [ebuf_free[1]] if ebuf_free[1][0] else [],
                [], []]
    pf_banks = [3, 4]
    vxb_last = {}
    dst_free = [(None, 0), (None, 0), (None, 0), (None, 0)]
    dslot = [ap["dstage"][:, 0, :], ap["dstage"][:, 1, :],
             ap["mx_buf"][:, 0, :], ap["mx_buf"][:, 1, :]]
    dsem = ["sb0", "sb1", "sb2", "sb3"]
    tile_store = {}
    rep_vv_fb = {}
    pf_lv = {}
    outn_done = {}

    wo23_lv = {}

    def pf_prolog(fb):
        sl = fb % 2
        pf_lv[fb] = wo_lv[fb] if fb in wo_lv else wo23_lv[fb]
        repdst = ("rep", "sqbuf")[fb % 2]
        if fb == 0:
            rg = [("vv", pb_marks["vv"])]
        elif fb == 1:
            rg = [("vv", vfl)]
        else:
            rg = [("vv", vxb_last[fb - 2])]
        rep_vv_fb[fb] = (repdst, build_rep_pf(fb, repdst, rg))

    def pf_tile(fb, ct):
        sl = fb % 2
        lva = pf_lv[fb]
        wname = "wobuf" if fb < 2 else "wgbuf"
        wsem = f"wo{sl}" if fb < 2 else f"wo{fb}"
        repdst, rep_vv = rep_vv_fb[fb]
        u = fb * NT + ct
        xsl = u % 4
        xw = xst_free[xsl]
        xl = E.op("sync", lambda e, ct=ct, fb=fb, xsl=xsl: e.dma_start(
            out=xslot[xsl], in_=x_r[:, ct, fb * 512:(fb + 1) * 512]),
            sem=xsem[xsl], inc=16, waits=xw)
        vxb = E.op("vector", lambda e, xsl=xsl, repdst=repdst: e.tensor_tensor(
            xslot[xsl], xslot[xsl], ap[repdst], ALU.add),
            sem="vv", waits=[(xsem[xsl], xl), ("vv", rep_vv)])
        vxb_last[fb] = vxb
        bank = pf_banks[u % 2] if fb == 0 else (1, 2, 3, 4)[u % 4]
        w2 = [(wsem, lva), outn_done[ct]]
        for k in range(NF):
            bld = (lambda e, k=k, sl=sl, ct=ct, bank=bank, wname=wname: e.matmul(
                pbF[bank][:, 0:512], ap["outnT"][:, k, ct * P:(ct + 1) * P],
                ap[wname][:, sl, k, :], start=(k == 0), stop=(k == NF - 1)))
            if k == 0:
                tv = t_first(bank, bld, waits=w2)
            else:
                tv = t_more(bld, sem="tt" if k == NF - 1 else None)
        dsl = u % 2 if fb == 0 else u % 4
        if dsl >= 2 and dst_free[dsl][0] is None:
            dw = [mx_free[dsl - 2]]
        else:
            dw = [dst_free[dsl]] if dst_free[dsl][0] else []
        dsv = dslot[dsl]
        vst = E.op("vector", lambda e, bank=bank, dsv=dsv, xsl=xsl: e.tensor_tensor(
            dsv, pbF[bank][:, 0:512], xslot[xsl], ALU.add),
            sem="vv", waits=[("tt", tv), ("vv", vxb)] + dw)
        consume(bank, "vv", vst)
        xst_free[xsl] = [("vv", vst)]
        sv = E.op("sync" if fb == 0 else "scalar",
                  lambda e, ct=ct, fb=fb, dsv=dsv: e.dma_start(
                      out=out_r[:, ct, fb * 512:(fb + 1) * 512], in_=dsv),
                  sem=dsem[dsl], inc=16, waits=[("vv", vst)])
        dst_free[dsl] = (dsem[dsl], sv)
        tile_store[(ct, fb)] = (dsem[dsl], sv)

    pf_prolog(0)
    pf_prolog(1)

    E.wait("scalar", [("tt", pc_marks["tt"])])
    E.wait("vector", [("tt", pc_marks["tt"]), ("ss", pc_marks["ss"])])
    zb = 2
    tvz = None
    for n in range(NCH):
        for h in range(H):
            i = n * H + h
            bld = (lambda e, n=n, h=h, i=i: e.matmul(
                pbF[zb][:, i:i + 1], ap["qfT"][:, h, n * P:(n + 1) * P],
                ap["S_bf"][:, h, D:D + 1], start=True, stop=True))
            if i == 0:
                tvz = t_first(zb, bld, waits=[("vv", sbf_pre)])
            else:
                tvz = t_more(bld, sem="tt" if i == NCH * H - 1 else None)
    v_zf = E.op("vector", lambda e: e.scalar_tensor_tensor(
        ap["z_sb"][:, 0:NCH * H], pbF[zb][:, 0:NCH * H], EPS_AT,
        ap["z_sb"][:, 0:NCH * H], ALU.add, ALU.add),
        sem="vv", waits=[("tt", tvz)])
    consume(zb, "vv", v_zf)
    v_rzf = E.op("vector", lambda e: e.reciprocal(
        ap["n1"][:, 0:NCH * H], ap["z_sb"][:, 0:NCH * H]),
        sem="vv", waits=[("vv", v_zf)])
    mx_free = [(None, 0), (None, 0)]
    hspb_free = [(None, 0), (None, 0)]
    mx4 = ap["mx_buf"].rearrange("p s (j c) -> p s j c", c=P)
    pe_pend = []

    def emit_pe_transpose(item):
        n, j4, sl, sc_marks = item
        tvt = None
        for jj in range(4):
            bld = (lambda e, jj=jj, sl=sl: e.transpose(
                pbB[6 + sl][:, jj * P:(jj + 1) * P],
                ap["hsp_bf"][:, sl, jj * P:(jj + 1) * P], ap["ident"]))
            if jj == 0:
                tvt = t_first(6 + sl, bld, waits=sc_marks)
            else:
                tvt = t_more(bld, sem="tt" if jj == 3 else None)
        hspb_free[sl] = ("tt", tvt)
        dsts = ap["outnT"][:, 4 * j4:4 * j4 + 4, n * P:(n + 1) * P]
        srcs = pbB[6 + sl].rearrange("p (j c) -> p j c", c=P)[:, 0:4, :]
        s_ot = E.op("scalar", lambda e, dsts=dsts, srcs=srcs: e.copy(
            dsts, srcs), sem="ss", waits=[("tt", tvt)])
        consume(6 + sl, "ss", s_ot)
        if j4 == 3:
            outn_done[n] = ("ss", s_ot)

    for n in range(NCH):
        for j4 in range(4):
            g = n * 4 + j4
            sl = g % 2
            tva = None
            for jj in range(4):
                h = j4 * 4 + jj
                bld = (lambda e, n=n, h=h, jj=jj, sl=sl: e.matmul(
                    pbF[sl][:, jj * P:(jj + 1) * P],
                    ap["qfT"][:, h, n * P:(n + 1) * P],
                    ap["S_bf"][:, h, 0:D], start=True, stop=True))
                if jj == 0:
                    tva = t_first(sl, bld, waits=[("vv", sbf_pre)])
                else:
                    tva = t_more(bld, sem="tt" if jj == 3 else None)
            if len(pe_pend) >= 2:
                emit_pe_transpose(pe_pend.pop(0))
            mw = [mx_free[sl]] if mx_free[sl][0] else []
            v_up = E.op("vector", lambda e, n=n, j4=j4, sl=sl: e.tensor_copy(
                mx4[:, sl, :, :], ap["numer"][:, n, 4 * j4:4 * j4 + 4, :]),
                sem="vv", waits=mw)
            v_add = E.op("vector", lambda e, sl=sl: e.tensor_tensor(
                ap["mx_buf"][:, sl, :], pbF[sl][:, 0:512],
                ap["mx_buf"][:, sl, :], ALU.add),
                sem="vv", waits=[("vv", v_up), ("tt", tva)])
            consume(sl, "vv", v_add)
            hw = [hspb_free[sl]] if hspb_free[sl][0] else []
            sc_marks = []
            for jj in range(4):
                i = n * H + j4 * 4 + jj
                dst = ap["hsp_bf"][:, sl, jj * P:(jj + 1) * P]
                srcm = mx4[:, sl, jj, :]
                if jj % 2 == 0:
                    sc = E.op("scalar", lambda e, dst=dst, srcm=srcm, i=i:
                              e.activation(dst, srcm, AF.Copy,
                                           scale=ap["n1"][:, i:i + 1]),
                              sem="ss",
                              waits=[("vv", v_add), ("vv", v_rzf)] + hw)
                    sc_marks.append(("ss", sc))
                else:
                    sc = E.op("vector", lambda e, dst=dst, srcm=srcm, i=i:
                              e.tensor_scalar(dst, srcm,
                                              ap["n1"][:, i:i + 1], None,
                                              ALU.mult),
                              sem="vv",
                              waits=[("vv", v_add), ("vv", v_rzf)] + hw)
                    sc_marks.append(("vv", sc))
                hw = []
            mx_free[sl] = sc_marks[-1]
            pe_pend.append((n, j4, sl, sc_marks))
        if n > 1:
            pf_tile(0, n - 2)
    while pe_pend:
        emit_pe_transpose(pe_pend.pop(0))
    pf_tile(0, NCH - 2)
    pf_tile(0, NCH - 1)
    pe_marks = E.marks()
    dump("outnT", ap["outnT"], pe_marks)

    if phases_upto == "Pe":
        finalize(nc, E)
        return nc, dump_outs

    for fb in (2, 3):
        sl = fb % 2
        wo23_lv[fb] = E.op("gpsimd", lambda e, fb=fb, sl=sl: e.dma_start(
            out=ap["wgbuf"][:, sl, :, :],
            in_=wo_r[:, :, fb * 512:(fb + 1) * 512]),
            sem=f"wo{fb}", inc=16, waits=[("tt", pe_marks["tt"])])
    for ct in range(NT):
        pf_tile(1, ct)
    pf_prolog(2)
    for ct in range(NT):
        pf_tile(2, ct)
    wg0_free = ("tt", E.val("tt"))
    pf_prolog(3)
    for ct in range(NT):
        pf_tile(3, ct)
    wg1_free = ("tt", E.val("tt"))
    pf_marks = E.marks()
    ebuf_free[0] = xst_free[0][0]
    ebuf_free[1] = xst_free[1][0]
    obuf_pf_free = [xst_free[2][0], xst_free[3][0]]

    if phases_upto == "Pf":
        finalize(nc, E)
        return nc, dump_outs

    warm(AF.Silu, ap["wsc"])
    wg_free = [wg0_free, wg1_free]
    wu_free = [("vv", pe_marks["vv"]), ("vv", pe_marks["vv"])]
    wd_freeA = [("tt", pf_marks["tt"]), ("tt", pf_marks["tt"])]
    wd_freeB = [("tt", pf_marks["tt"]), ("tt", pf_marks["tt"])]
    oslot = [ap["obuf"][:, 0, :], ap["obuf"][:, 1, :],
             ap["mx_buf"][:, 0, :], ap["mx_buf"][:, 1, :]]
    osem = ["xo0", "xo1", "xe0", "xe1"]
    obuf_free = [obuf_pf_free[0], obuf_pf_free[1],
                 dst_free[2] if dst_free[2][0] else mx_free[0],
                 dst_free[3] if dst_free[3][0] else mx_free[1]]
    E.wait("vector", [("ss", pe_marks["ss"])])
    hspbf_free = [("tt", pe_marks["tt"]), ("tt", pe_marks["tt"])]
    pend = []
    uglob = [0]
    hsp_state = {"ga": ("tt", pf_marks["tt"]), "gb": ("tt", pf_marks["tt"])}

    def emit_hsp_transpose(item):
        u, ib8_, ct_, v_sp_ = item
        usl = u % 2
        trb = 6 + (u % 2)
        for j in range(4):
            bld = (lambda e, j=j, trb=trb, usl=usl: e.transpose(
                pbB[trb][:, j * P:(j + 1) * P],
                ap["hsp_bf"][:, usl, j * P:(j + 1) * P], ap["ident"]))
            if j == 0:
                tvt = t_first(trb, bld, waits=[("vv", v_sp_)])
            else:
                tvt = t_more(bld, sem="tt" if j == 3 else None)
        hspbf_free[usl] = ("tt", tvt)
        s0 = (ib8_ % 4) * 4
        dstn = "hspT_a" if ib8_ < 4 else "hspT_b"
        dsts = ap[dstn][:, s0:s0 + 4, ct_ * P:(ct_ + 1) * P]
        srcs = pbB[trb].rearrange("p (j c) -> p j c", c=P)[:, 0:4, :]
        g = hsp_state["ga" if ib8_ < 4 else "gb"]
        s_cp = E.op("scalar", lambda e, dsts=dsts, srcs=srcs: e.copy(dsts, srcs),
                    sem="ss", waits=[("tt", tvt), g])
        consume(trb, "ss", s_cp)

    for grp in range(2):
        for ib8 in range(8):
            ibg = grp * 8 + ib8
            reg = ibg % 2
            iofs = ibg * 512
            wgw = [wg_free[reg]] if wg_free[reg][0] else []
            lg = E.op("gpsimd", lambda e, reg=reg, iofs=iofs: e.dma_start(
                out=ap["wgbuf"][:, reg, :, :],
                in_=wg_r[:, :, iofs:iofs + 512]),
                sem=f"wg{reg}", inc=16, waits=wgw)
            wuw = [wu_free[reg]] if wu_free[reg][0] else []
            lu = E.op("gpsimd", lambda e, reg=reg, iofs=iofs: e.dma_start(
                out=ap["wubuf"][:, reg, :, :],
                in_=wu_r[:, :, iofs:iofs + 512]),
                sem=f"wu{reg}", inc=16, waits=wuw)
            for ct in range(NT):
                csl = ct % 2
                gb = csl * 2
                ub = csl * 2 + 1
                for k in range(NF):
                    bld = (lambda e, k=k, reg=reg, ct=ct, gb=gb: e.matmul(
                        pbF[gb][:, 0:512],
                        ap["xnT"][:, k, ct * P:(ct + 1) * P],
                        ap["wgbuf"][:, reg, k, :],
                        start=(k == 0), stop=(k == NF - 1)))
                    if k == 0:
                        tvg = t_first(gb, bld, waits=[(f"wg{reg}", lg), ("ss", m_xn["ss"])])
                    else:
                        tvg = t_more(bld, sem="tt" if k == NF - 1 else None)
                for k in range(NF):
                    bld = (lambda e, k=k, reg=reg, ct=ct, ub=ub: e.matmul(
                        pbF[ub][:, 0:512],
                        ap["xnT"][:, k, ct * P:(ct + 1) * P],
                        ap["wubuf"][:, reg, k, :],
                        start=(k == 0), stop=(k == NF - 1)))
                    if k == 0:
                        tv = t_first(ub, bld, waits=[(f"wu{reg}", lu)])
                    else:
                        tv = t_more(bld, sem="tt" if k == NF - 1 else None)
                ew0 = [ebuf_free[0]] if ebuf_free[0][0] else []
                s_si = E.op("scalar", lambda e, gb=gb: e.activation(
                    ap["e_buf"][:, 0, :], pbF[gb][:, 0:512], AF.Silu),
                    sem="ss", waits=[("tt", tv)] + ew0)
                consume(gb, "ss", s_si)
                ew1 = [ebuf_free[1]] if ebuf_free[1][0] else []
                v_h = E.op("vector", lambda e, ub=ub: e.tensor_tensor(
                    ap["e_buf"][:, 1, :], ap["e_buf"][:, 0, :], pbF[ub][:, 0:512],
                    ALU.mult), sem="vv", waits=[("ss", s_si)] + ew1)
                ebuf_free[0] = ("vv", v_h)
                consume(ub, "vv", v_h)
                sq = ap["sqbuf"].rearrange("p (q f) -> p q f", f=4)
                gq = ap["geb"].rearrange("p (q f) -> p q f", f=4)
                E.op("vector", lambda e: e.tensor_mul(
                    ap["sqbuf"], ap["e_buf"][:, 1, :], ap["e_buf"][:, 1, :]),
                    waits=[("vv", v_h)])
                E.op("vector", lambda e, sq=sq: e.tensor_tensor(
                    ap["m1"], sq[:, :, 0], sq[:, :, 1], ALU.max))
                E.op("vector", lambda e, sq=sq: e.tensor_tensor(
                    ap["n1"], sq[:, :, 0], sq[:, :, 1], ALU.min))
                E.op("vector", lambda e, sq=sq: e.tensor_tensor(
                    ap["m2"], sq[:, :, 2], sq[:, :, 3], ALU.max))
                E.op("vector", lambda e, sq=sq: e.tensor_tensor(
                    ap["n2"], sq[:, :, 2], sq[:, :, 3], ALU.min))
                E.op("vector", lambda e: e.tensor_tensor(
                    ap["m1"], ap["m1"], ap["m2"], ALU.min))
                E.op("vector", lambda e: e.tensor_tensor(
                    ap["n1"], ap["n1"], ap["n2"], ALU.max))
                E.op("vector", lambda e: e.tensor_tensor(
                    ap["tq"], ap["m1"], ap["n1"], ALU.max))
                for j in range(4):
                    E.op("vector", lambda e, j=j, sq=sq, gq=gq: e.tensor_tensor(
                        gq[:, :, j], sq[:, :, j], ap["tq"], ALU.is_ge))
                usl = uglob[0] % 2
                hw_ = [hspbf_free[usl]] if hspbf_free[usl][0] else []
                v_sp = E.op("vector", lambda e, usl=usl: e.tensor_tensor(
                    ap["hsp_bf"][:, usl, :], ap["e_buf"][:, 1, :], ap["geb"],
                    ALU.mult), sem="vv", waits=hw_)
                ebuf_free[1] = ("vv", v_sp)
                pend.append((uglob[0], ib8, ct, v_sp))
                uglob[0] += 1
                if len(pend) > 1:
                    emit_hsp_transpose(pend.pop(0))
            wg_free[reg] = ("tt", E.val("tt"))
            wu_free[reg] = ("tt", E.val("tt"))
        while pend:
            emit_hsp_transpose(pend.pop(0))
        gu_marks = E.marks()
        if grp == 0:
            dump("hsa0", ap["hspT_a"], gu_marks)
            dump("hsb0", ap["hspT_b"], gu_marks)
        wd_lv = {}

        def emit_wd_loads(fb, extra, grp=grp):
            slw = fb % 2
            wA = ([wd_freeA[slw]] if wd_freeA[slw][0] else []) + [wu_free[slw]]
            lwA = E.op("gpsimd", lambda e, grp=grp, fb=fb, slw=slw: e.dma_start(
                out=ap["wdbuf"][:, slw, :, :],
                in_=wd_r[:, grp * 32: grp * 32 + 16,
                         fb * 512:(fb + 1) * 512]),
                sem=f"wd{slw}", inc=16, waits=wA + extra)
            wB = ([wd_freeB[slw]] if wd_freeB[slw][0] else []) + [wg_free[slw]]
            lwB = E.op("gpsimd", lambda e, grp=grp, fb=fb, slw=slw: e.dma_start(
                out=ap["wdbuf2"][:, slw, :, :],
                in_=wd_r[:, grp * 32 + 16: grp * 32 + 32,
                         fb * 512:(fb + 1) * 512]),
                sem=f"wd{2 + slw}", inc=16, waits=wB + extra)
            wd_lv[fb] = (slw, lwA, lwB)

        emit_wd_loads(0, [])
        emit_wd_loads(1, [])
        fb_end_tt = {}
        for fb in range(4):
            slw, lwA, lwB = wd_lv[fb]
            for ct in range(NT):
                u = fb * NT + ct
                bank = 2 + (u % 4)
                for s in range(32):
                    hsname = "hspT_a" if s < 16 else "hspT_b"
                    srow = s % 16
                    wbname = "wdbuf" if s < 16 else "wdbuf2"
                    bld = (lambda e, hsname=hsname, srow=srow, ct=ct,
                           bank=bank, slw=slw, wbname=wbname, s=s: e.matmul(
                               pbF[bank][:, 0:512],
                               ap[hsname][:, srow, ct * P:(ct + 1) * P],
                               ap[wbname][:, slw, srow, :],
                               start=(s == 0), stop=(s == 31)))
                    if s == 0:
                        tv = t_first(bank, bld,
                                waits=[(f"wd{slw}", lwA), ("ss", gu_marks["ss"])])
                    elif s == 16:
                        tv = t_more(bld, waits=[(f"wd{2 + slw}", lwB)])
                    else:
                        tv = t_more(bld, sem="tt" if s == 31 else None)
                osl = u % 4
                ow = [obuf_free[osl]] if obuf_free[osl][0] else []
                ol = E.op("sync", lambda e, ct=ct, fb=fb, osl=osl: e.dma_start(
                    out=oslot[osl],
                    in_=out_r[:, ct, fb * 512:(fb + 1) * 512]),
                    sem=osem[osl], inc=16,
                    waits=ow + [tile_store[(ct, fb)]])
                dsl = u % 2
                dw = [dst_free[dsl]] if dst_free[dsl][0] else []
                vad = E.op("vector", lambda e, bank=bank, dsl=dsl, osl=osl: e.tensor_tensor(
                    ap["dstage"][:, dsl, :], pbF[bank][:, 0:512],
                    oslot[osl], ALU.add),
                    sem="vv", waits=[("tt", tv), (osem[osl], ol)] + dw)
                consume(bank, "vv", vad)
                obuf_free[osl] = ("vv", vad)
                sv = E.op("sync", lambda e, ct=ct, fb=fb, dsl=dsl: e.dma_start(
                    out=out_r[:, ct, fb * 512:(fb + 1) * 512],
                    in_=ap["dstage"][:, dsl, :]),
                    sem=f"sb{dsl}", inc=16, waits=[("vv", vad)])
                dst_free[dsl] = (f"sb{dsl}", sv)
                tile_store[(ct, fb)] = (f"sb{dsl}", sv)
            fb_end_tt[fb] = E.val("tt")
            wd_freeA[slw] = ("tt", fb_end_tt[fb])
            wd_freeB[slw] = ("tt", fb_end_tt[fb])
            if fb + 2 < 4:
                emit_wd_loads(fb + 2, [])
        hsp_state["ga"] = ("tt", E.val("tt"))
        hsp_state["gb"] = ("tt", E.val("tt"))
        wu_free[0] = ("tt", fb_end_tt[2])
        wu_free[1] = ("tt", fb_end_tt[3])
        wg_free[0] = ("tt", fb_end_tt[2])
        wg_free[1] = ("tt", fb_end_tt[3])

    finalize(nc, E)
    return nc, dump_outs


def finalize(nc, E):
    E.wait("sync", [("sb0", E.val("sb0")), ("sb1", E.val("sb1")),
                    ("sb2", E.val("sb2")), ("sb3", E.val("sb3")),
                    ("sd", E.val("sd"))])
    with contextlib.ExitStack() as ctx:
        for s in list(E.cnt):
            E.handles[s] = ctx.enter_context(nc.semaphore(s))
        block = ctx.enter_context(nc.Block())

        def run(name):
            def body(eng):
                for f in E.ops[name]:
                    f(eng)
            return body

        block.sync(run("sync"))
        block.gpsimd(run("gpsimd"))
        block.tensor(run("tensor"))
        block.vector(run("vector"))
        block.scalar(run("scalar"))


def kernel(**inputs):
    x = np.ascontiguousarray(np.asarray(inputs["L0_attn_x"], dtype=np.float32))
    Wq = np.ascontiguousarray(np.asarray(inputs["L0_attn_Wq"], dtype=np.float32))
    Wk = np.ascontiguousarray(np.asarray(inputs["L0_attn_Wk"], dtype=np.float32))
    Wv = np.ascontiguousarray(np.asarray(inputs["L0_attn_Wv"], dtype=np.float32))
    Wo = np.ascontiguousarray(np.asarray(inputs["L0_attn_Wo"], dtype=np.float32))
    bq = np.ascontiguousarray(np.asarray(inputs["L0_attn_bq"], dtype=np.float32)).reshape(1, DM)
    bk = np.ascontiguousarray(np.asarray(inputs["L0_attn_bk"], dtype=np.float32)).reshape(1, DM)
    bv = np.ascontiguousarray(np.asarray(inputs["L0_attn_bv"], dtype=np.float32)).reshape(1, DM)
    bo = np.ascontiguousarray(np.asarray(inputs["L0_attn_bo"], dtype=np.float32)).reshape(1, DM)
    wg = np.ascontiguousarray(np.asarray(inputs["L0_ffn_w_gate"], dtype=np.float32))
    wu = np.ascontiguousarray(np.asarray(inputs["L0_ffn_w_up"], dtype=np.float32))
    wd = np.ascontiguousarray(np.asarray(inputs["L0_ffn_w_down"], dtype=np.float32))

    R = 1024
    nc, _ = build_program(R=R)
    in_maps = []
    for c in range(8):
        s, hf = c // 2, c % 2
        fl = np.empty((P, 2), np.float32)
        fl[:, 0] = float(hf)
        fl[:, 1] = float(1 - hf)
        in_maps.append({
            "x": np.ascontiguousarray(x[s, hf * R:(hf + 1) * R]),
            "flag": fl,
            "wq": Wq, "wk": Wk, "wv": Wv, "wo": Wo,
            "bq": bq, "bk": bk, "bv": bv, "bo": bo,
            "wg": wg, "wu": wu, "wd": wd,
        })
    res = run_bass_kernel_spmd(nc, in_maps, core_ids=list(range(8)))
    out = np.empty((4, 2048, DM), np.float32)
    for c in range(8):
        s, hf = c // 2, c % 2
        out[s, hf * R:(hf + 1) * R] = res.results[c]["out"]
    return out
